# version 17
# baseline (speedup 1.0000x reference)
import math
import ml_dtypes
import numpy as np
from contextlib import ExitStack

import concourse.bass as bass
import concourse.mybir as mybir
from concourse.bass_utils import run_bass_kernel_spmd

F32 = mybir.dt.float32
BF16 = mybir.dt.bfloat16
FP8 = mybir.dt.float8e4
I32 = mybir.dt.int32
A = mybir.AluOpType
AF = mybir.ActivationFunctionType
AX = mybir.AxisListType

T = 4096
C = 1024
HID = 4096
NH = 16
HD = 64
CH = 512
NCH = T // CH
LN2_ = math.log(2.0)
EPS = 1e-5
QB = 1024


class Ctr:
    def __init__(self, sem, step=1):
        self.sem = sem
        self.step = step
        self.n = 0

    def inc(self, instr):
        instr.then_inc(self.sem, self.step)
        self.n += self.step
        return self.n


def build(nc: bass.Bass, es: ExitStack):
    dram = lambda n, sh, dt, k: nc.dram_tensor(n, sh, dt, kind=k).ap()
    x_h = dram("x", [T, C], F32, "ExternalInput")
    wqkv_h = dram("w_qkv", [3 * C, C], F32, "ExternalInput")
    wproj_h = dram("w_proj", [C, C], F32, "ExternalInput")
    bproj_h = dram("b_proj", [C], F32, "ExternalInput")
    w1_h = dram("w1", [HID, C], F32, "ExternalInput")
    b1_h = dram("b1", [HID], F32, "ExternalInput")
    w2_h = dram("w2", [C, HID], F32, "ExternalInput")
    b2_h = dram("b2", [C], F32, "ExternalInput")
    ident_h = dram("ident", [128, 128], BF16, "ExternalInput")
    out_h = dram("out", [T, C], F32, "ExternalOutput")
    qkvscr = dram("qkvscr", [3 * C, C], BF16, "Internal")
    projscr = dram("projscr", [C, C], BF16, "Internal")
    w1scr = dram("w1scr", [HID, C], BF16, "Internal")
    w1T_scr = dram("w1T_scr", [C, HID], BF16, "Internal")
    wqscr = dram("wqscr", [C, HID], BF16, "Internal")
    wqT8h = dram("wqT8h", [HID, C], FP8, "Internal")

    sb = lambda n, sh, dt: es.enter_context(nc.sbuf_tensor(n, sh, dt))

    wqT = [sb(f"wqT{j}", [128, C], BF16) for j in range(8)]
    wpT = [sb(f"wpT{j}", [128, C], BF16) for j in range(8)]
    wq8buf = sb("wq8buf", [128, 4 * C], FP8)
    wq8stg = sb("wq8stg", [128, C], FP8)
    ctxn = [sb(f"ctxn{g}", [128, 128], BF16) for g in range(8)]
    b1sb = sb("b1sb", [128, 32], F32)
    identT = sb("identT", [128, 128], BF16)
    bprojrep = sb("bprojrep", [128, C], BF16)
    bqrep = sb("bqrep", [128, C], F32)
    statR = sb("statR", [128, 32], F32)
    statB = sb("statB", [128, 32], F32)
    ones_col = sb("ones_col", [128, 8], BF16)
    con = sb("con", [128, 12], F32)

    xc = sb("xc", [128, 4 * C], F32)
    lnc = sb("lnc", [128, 4 * C], BF16)
    lnT = sb("lnT", [128, 4 * CH], BF16)
    lnT2 = sb("lnT2", [128, 4 * CH], BF16)
    eq_ = sb("eq", [128, 4 * C], BF16)
    qsT = sb("qsT", [128, 4 * CH], BF16)
    qsT2 = sb("qsT2", [128, 4 * CH], BF16)
    dq = sb("dq", [128, 16], F32)
    rdq = sb("rdq", [128, 16], F32)
    yc = sb("yc", [128, 4 * C], F32)
    st4 = sb("st4", [128, 8], F32)
    bnbuf = sb("bnbuf", [128, 12], F32)
    hbuf = sb("hbuf", [128, HID * CH // 256], BF16)
    hq8 = sb("hq8", [128, HID * CH // 128], FP8)
    w1buf = sb("w1buf", [128, 2 * HID], BF16)
    q_l = sb("q_l", [128, QB], F32)
    q_i0 = sb("q_i0", [128, QB], I32)
    q_ec = sb("q_ec", [128, QB], BF16)
    q_inv = sb("q_inv", [128, QB], BF16)
    q_scl = sb("q_scl", [128, QB], BF16)
    q_t = sb("q_t", [128, QB], F32)
    q_d = sb("q_d", [128, QB], F32)
    q_w = sb("q_w", [128, QB], BF16)
    q_w0 = sb("q_w0", [128, QB], BF16)
    q_c = sb("q_c", [128, QB], BF16)
    q_t16 = sb("q_t16", [128, QB], BF16)
    q_t162 = sb("q_t162", [128, QB], BF16)
    q_l2 = sb("q_l2", [128, QB], F32)
    q_t2 = sb("q_t2", [128, QB], F32)
    q_f16 = sb("q_f16", [128, QB], BF16)
    q_r = sb("q_r", [128, QB], BF16)
    ek = eq_[:, 0:C]
    vv = eq_[:, C:2 * C]
    qo = xc
    w2f = yc
    q_dm = q_d
    q_i5 = q_i0
    q_f = q_l
    q_u2 = q_l2
    q_a = q_t2
    wkvT = ([w1buf[:, j * 2 * C:(j + 1) * 2 * C] for j in range(4)]
            + [hbuf[:, j * 2 * C:(j + 1) * 2 * C] for j in range(4)])

    p_kv = es.enter_context(nc.psum_tensor("p_kv", [128, 2 * C], F32))
    p_ctx = es.enter_context(nc.psum_tensor("p_ctx", [128, C], F32))
    p_den = es.enter_context(nc.psum_tensor("p_den", [128, 512], F32))
    p_tr = es.enter_context(nc.psum_tensor("p_tr", [128, 512], F32))
    p_m1 = p_kv
    p_q = p_ctx

    sem = lambda n: es.enter_context(nc.semaphore(n))
    vC = Ctr(sem("s_v"))
    sC = Ctr(sem("s_s"))
    tC = Ctr(sem("s_t"))

    prog = {"sync": [], "scalar": [], "vector": [], "tensor": [], "gpsimd": []}

    def emit(eng, fn):
        prog[eng].append(fn)

    def wait(eng, ctr, val):
        if val > 0:
            emit(eng, lambda e, s=ctr.sem, v=val: e.wait_ge(s, v))

    def einc(eng, ctr, fn):
        ctr.n += ctr.step
        emit(eng, lambda e, fn=fn, s=ctr.sem, st=ctr.step: fn(e).then_inc(s, st))
        return ctr.n

    class DPool:

        def __init__(self, eng, k, tag):
            self.eng = eng
            self.ctrs = [Ctr(sem(f"{tag}{i}"), 16) for i in range(k)]
            self.idx = 0

        def group(self):
            c = self.ctrs[self.idx % len(self.ctrs)]
            self.idx += 1
            if c.n > 0:
                wait(self.eng, c, c.n)
            return c

    syP = DPool("sync", 8, "dy")
    scP = DPool("scalar", 8, "dc")
    gpP = DPool("gpsimd", 8, "dg")
    wqS = [Ctr(sem(f"wq{i}"), 16) for i in range(4)]

    def c_ap(j):
        return con[:, j:j + 1]

    for j, val in enumerate([1e-38, 9.0 * LN2_, -9.0 * LN2_, -248.0e30,
                             10.0 * LN2_, -10.0 * LN2_, -65520.0e30, EPS,
                             3.0 * LN2_, -3.0 * LN2_, 24.0 * LN2_, -24.0 * LN2_]):
        emit("vector", lambda v, j=j, val=val: v.memset(con[:, j:j + 1], val))
    einc("vector", vC, lambda v: v.memset(ones_col[:, :], 1.0))
    V_CONST = vC.n
    wait("scalar", vC, V_CONST)
    wait("tensor", vC, V_CONST)
    wait("gpsimd", vC, V_CONST)
    wait("sync", vC, V_CONST)

    def vop(fn):
        einc("vector", vC, fn)
        wait("vector", vC, vC.n)
        return vC.n

    def sop(fn):
        einc("scalar", sC, fn)
        wait("scalar", sC, sC.n)
        return sC.n

    def lvl(wbuf, cbuf, k, n):
        sl = slice(0, n)
        vop(lambda v, k=k: v.tensor_scalar(cbuf[:, sl], wbuf[:, sl], float(k), -float(k), A.is_ge, A.mult))
        vop(lambda v: v.tensor_tensor(wbuf[:, sl], wbuf[:, sl], cbuf[:, sl], A.add))

    def lvl2(win, wout, cbuf, k, n):
        sl = slice(0, n)
        vop(lambda v, k=k: v.tensor_scalar(cbuf[:, sl], win[:, sl], float(k), -float(k), A.is_ge, A.mult))
        vop(lambda v: v.tensor_tensor(wout[:, sl], win[:, sl], cbuf[:, sl], A.add))

    def quant_chain(n, src_ap, dst_ap, mode):
        sl = slice(0, n)
        wait("scalar", vC, vC.n)
        wait("vector", sC, sC.n)
        if mode == "h":
            sop(lambda s: s.activation(q_l[:, sl], src_ap, AF.Ln, bias=c_ap(0), scale=1.0))
            wait("vector", sC, sC.n)
            vop(lambda v: v.tensor_scalar(q_w[:, sl], q_l[:, sl], 1.0 / LN2_, 6.0, A.mult, A.add))
            vop(lambda v: v.tensor_scalar(q_w[:, sl], q_w[:, sl], 0.0, None, A.max))
            vop(lambda v: v.tensor_copy(q_w0[:, sl], q_w[:, sl]))
            for k in (8, 4, 2, 1):
                lvl(q_w, q_c, k, n)
            vop(lambda v: v.scalar_tensor_tensor(q_ec[:, sl], q_w0[:, sl], -9.0, q_w[:, sl], A.add, A.subtract))
            wait("scalar", vC, vC.n)
            einc("scalar", sC, lambda s: s.activation(q_inv[:, sl], q_ec[:, sl], AF.Exp, bias=0.0, scale=-LN2_))
            sop(lambda s: s.activation(q_scl[:, sl], q_ec[:, sl], AF.Exp, bias=0.0, scale=LN2_))
            wait("vector", sC, sC.n)
            vop(lambda v: v.tensor_tensor(q_t16[:, sl], src_ap, q_inv[:, sl], A.mult))
            vop(lambda v: v.tensor_scalar(q_w[:, sl], q_t16[:, sl], 0.5, 0.25 + 1e-13, A.mult, A.add))
            vop(lambda v: v.tensor_copy(q_w0[:, sl], q_w[:, sl]))
            for k in (8, 4, 2, 1):
                lvl(q_w, q_c, k, n)
            vop(lambda v: v.tensor_scalar(q_c[:, sl], q_w[:, sl], 0.5 + 1e-12, None, A.is_ge))
            vop(lambda v: v.tensor_tensor(q_r[:, sl], q_w0[:, sl], q_w[:, sl], A.subtract))
            vop(lambda v: v.scalar_tensor_tensor(q_r[:, sl], q_r[:, sl], 2.0, q_c[:, sl], A.mult, A.add))
            vop(lambda v: v.tensor_tensor(q_t16[:, sl], q_r[:, sl], q_scl[:, sl], A.mult))
            wait("scalar", vC, vC.n)
            sop(lambda s: s.activation(q_f16[:, sl], q_t16[:, sl], AF.Exp, bias=c_ap(3), scale=1.0e30))
            wait("vector", sC, sC.n)
            vop(lambda v: v.scalar_tensor_tensor(dst_ap, q_f16[:, sl], 1.0, q_t16[:, sl], A.add, A.mult))
            return
        halfeven = mode in ("w", "b")
        if mode == "o":
            sop(lambda s: s.activation(q_d[:, sl], src_ap, AF.Abs, bias=0.0, scale=1.0))
            sop(lambda s: s.activation(q_l[:, sl], q_d[:, sl], AF.Ln, bias=c_ap(0), scale=1.0))
            wait("vector", sC, sC.n)
            vop(lambda v: v.tensor_scalar(q_w[:, sl], q_l[:, sl], 1.0 / LN2_, 14.0, A.mult, A.add))
            vop(lambda v: v.tensor_scalar(q_w[:, sl], q_w[:, sl], 0.0, 30.0, A.max, A.min))
            vop(lambda v: v.tensor_copy(q_w0[:, sl], q_w[:, sl]))
            for k in (16, 8, 4, 2, 1):
                lvl(q_w, q_c, k, n)
            vop(lambda v: v.scalar_tensor_tensor(q_ec[:, sl], q_w0[:, sl], -24.0, q_w[:, sl], A.add, A.subtract))
            wait("scalar", vC, vC.n)
            einc("scalar", sC, lambda s: s.activation(q_inv[:, sl], q_ec[:, sl], AF.Exp, bias=0.0, scale=-LN2_))
            sop(lambda s: s.activation(q_scl[:, sl], q_ec[:, sl], AF.Exp, bias=0.0, scale=LN2_))
            wait("vector", sC, sC.n)
            vop(lambda v: v.tensor_tensor(q_t[:, sl], src_ap, q_inv[:, sl], A.mult))
            vop(lambda v: v.tensor_scalar(q_l[:, sl], q_t[:, sl], 0.5, 1024.25 + 1e-7, A.mult, A.add))
            vop(lambda v: v.tensor_copy(q_d[:, sl], q_l[:, sl]))
            for k in (2048, 1024, 512):
                lvl(q_d, q_c, k, n)
            vop(lambda v: v.tensor_scalar(q_c[:, sl], q_d[:, sl], 256.0, -256.0, A.is_ge, A.mult))
            vop(lambda v: v.tensor_tensor(q_w[:, sl], q_d[:, sl], q_c[:, sl], A.add))
            for k in (128, 64, 32, 16, 8, 4, 2, 1):
                lvl(q_w, q_c, k, n)
            vop(lambda v: v.tensor_scalar(q_c[:, sl], q_w[:, sl], 0.5 + 3e-7, None, A.is_ge))
            vop(lambda v: v.tensor_tensor(q_d[:, sl], q_l[:, sl], q_w[:, sl], A.subtract))
            vop(lambda v: v.scalar_tensor_tensor(q_d[:, sl], q_d[:, sl], 2.0, q_c[:, sl], A.mult, A.add))
            vop(lambda v: v.scalar_tensor_tensor(q_t[:, sl], q_d[:, sl], -2048.0, q_scl[:, sl], A.add, A.mult))
            wait("scalar", vC, vC.n)
            sop(lambda s: s.activation(q_d[:, sl], q_t[:, sl], AF.Abs, bias=0.0, scale=1.0))
            sop(lambda s: s.activation(q_f16[:, sl], q_d[:, sl], AF.Exp, bias=c_ap(6), scale=1.0e30))
            wait("vector", sC, sC.n)
            vop(lambda v: v.scalar_tensor_tensor(dst_ap, q_f16[:, sl], 1.0, q_t[:, sl], A.add, A.mult))
            return
        man3 = mode == "w"
        cbias_inv = 8 if man3 else 4
        cbias_scl = 9 if man3 else 5
        cmin = -6.0 if man3 else -14.0
        covf = 3 if man3 else 6
        shift = 32.0 if man3 else 2048.0
        cc = 1.0 / (2.0 * LN2_)
        man_ = 3.0 if man3 else 10.0
        sop(lambda s: s.activation(q_t[:, sl], src_ap, AF.Square, bias=0.0, scale=1.0))
        sop(lambda s: s.activation(q_l[:, sl], q_t[:, sl], AF.Ln, bias=c_ap(0), scale=1.0))
        wait("vector", sC, sC.n)
        vop(lambda v: v.tensor_scalar(q_i0[:, sl], q_l[:, sl], cc, 63.5, A.mult, A.add))
        vop(lambda v: v.scalar_tensor_tensor(q_dm[:, sl], q_l[:, sl], cc, q_i0[:, sl], A.mult, A.subtract))
        vop(lambda v: v.scalar_tensor_tensor(q_ec[:, sl], q_dm[:, sl], -63.0, q_i0[:, sl], A.is_ge, A.add))
        vop(lambda v: v.tensor_scalar(q_i0[:, sl], q_ec[:, sl], -64.0 - man_, cmin - man_, A.add, A.max))
        wait("scalar", vC, vC.n)
        einc("scalar", sC, lambda s: s.activation(q_inv[:, sl], q_i0[:, sl], AF.Exp, bias=0.0, scale=-LN2_))
        sop(lambda s: s.activation(q_scl[:, sl], q_i0[:, sl], AF.Exp, bias=0.0, scale=LN2_))
        wait("vector", sC, sC.n)
        vop(lambda v: v.tensor_tensor(q_t[:, sl], src_ap, q_inv[:, sl], A.mult))
        vop(lambda v: v.tensor_scalar(q_t[:, sl], q_t[:, sl], shift, None, A.add))
        vop(lambda v: v.tensor_scalar(q_i5[:, sl], q_t[:, sl], 0.5, -0.5, A.mult, A.add))
        vop(lambda v: v.scalar_tensor_tensor(q_d[:, sl], q_t[:, sl], 0.5, q_i5[:, sl], A.mult, A.subtract))
        vop(lambda v: v.scalar_tensor_tensor(q_d[:, sl], q_d[:, sl], 1.0, q_i5[:, sl], A.is_ge, A.add))
        vop(lambda v: v.scalar_tensor_tensor(q_u2[:, sl], q_d[:, sl], -2.0, q_t[:, sl], A.mult, A.add))
        vop(lambda v: v.tensor_scalar(q_a[:, sl], q_u2[:, sl], 0.5 + 1e-9, None, A.is_gt))
        vop(lambda v: v.tensor_tensor(q_d[:, sl], q_t[:, sl], q_u2[:, sl], A.subtract))
        vop(lambda v: v.tensor_tensor(q_d[:, sl], q_d[:, sl], q_a[:, sl], A.add))
        vop(lambda v: v.tensor_scalar(q_a[:, sl], q_u2[:, sl], 1.5 - 1e-9, None, A.is_ge))
        vop(lambda v: v.tensor_tensor(q_d[:, sl], q_d[:, sl], q_a[:, sl], A.add))
        vop(lambda v: v.scalar_tensor_tensor(q_t[:, sl], q_d[:, sl], -shift, q_scl[:, sl], A.add, A.mult))
        wait("scalar", vC, vC.n)
        sop(lambda s: s.activation(q_d[:, sl], q_t[:, sl], AF.Abs, bias=0.0, scale=1.0))
        sop(lambda s: s.activation(q_f[:, sl], q_d[:, sl], AF.Exp, bias=c_ap(covf), scale=1.0e30))
        wait("vector", sC, sC.n)
        vop(lambda v: v.scalar_tensor_tensor(dst_ap, q_f[:, sl], 1.0, q_t[:, sl], A.add, A.mult))

    def quant_chain_h_list(items, pre_emit=None):
        NBat = len(items)
        vm1, vm2, vm3, sm1, sm2, sm3 = {}, {}, {}, {}, {}, {}
        QL = (q_l, q_l2)
        QT = (q_t16, q_t162)
        sl = slice(0, QB)
        entry_v = vC.n
        if pre_emit is None:
            pre_emit = lambda i: None

        def S1(i):
            ql = QL[i % 2]
            s0 = items[i][0]
            wait("scalar", vC, vm1.get(i - 2, entry_v))
            sop(lambda s, ql=ql, s0=s0: s.activation(ql[:, sl], hbuf[:, s0:s0 + QB], AF.Ln, bias=c_ap(0), scale=1.0))
            sm1[i] = sC.n

        def V1(i):
            ql = QL[i % 2]
            wait("vector", sC, max(sm1[i], sm2.get(i - 1, 0)))
            vop(lambda v, ql=ql: v.tensor_scalar(q_w0[:, sl], ql[:, sl], 1.0 / LN2_, 6.0, A.mult, A.add))
            vop(lambda v: v.tensor_scalar(q_w0[:, sl], q_w0[:, sl], 0.0, None, A.max))
            lvl2(q_w0, q_w, q_c, 8, QB)
            for k in (4, 2, 1):
                lvl(q_w, q_c, k, QB)
            vop(lambda v: v.scalar_tensor_tensor(q_ec[:, sl], q_w0[:, sl], -9.0, q_w[:, sl], A.add, A.subtract))
            vm1[i] = vC.n

        def S2(i):
            wait("scalar", vC, vm1[i])
            einc("scalar", sC, lambda s: s.activation(q_inv[:, sl], q_ec[:, sl], AF.Exp, bias=0.0, scale=-LN2_))
            sop(lambda s: s.activation(q_scl[:, sl], q_ec[:, sl], AF.Exp, bias=0.0, scale=LN2_))
            sm2[i] = sC.n

        def V2(i):
            qt = QT[i % 2]
            s0 = items[i][0]
            wait("vector", sC, sm2[i])
            vop(lambda v, qt=qt, s0=s0: v.tensor_tensor(qt[:, sl], hbuf[:, s0:s0 + QB], q_inv[:, sl], A.mult))
            vop(lambda v, qt=qt: v.tensor_scalar(q_w0[:, sl], qt[:, sl], 0.5, 0.25 + 1e-13, A.mult, A.add))
            lvl2(q_w0, q_w, q_c, 8, QB)
            for k in (4, 2, 1):
                lvl(q_w, q_c, k, QB)
            vop(lambda v: v.tensor_scalar(q_c[:, sl], q_w[:, sl], 0.5 + 1e-12, None, A.is_ge))
            vop(lambda v: v.tensor_tensor(q_r[:, sl], q_w0[:, sl], q_w[:, sl], A.subtract))
            vop(lambda v: v.scalar_tensor_tensor(q_r[:, sl], q_r[:, sl], 2.0, q_c[:, sl], A.mult, A.add))
            vop(lambda v, qt=qt: v.tensor_tensor(qt[:, sl], q_r[:, sl], q_scl[:, sl], A.mult))
            vm2[i] = vC.n

        def S3(i):
            qt = QT[i % 2]
            wait("scalar", vC, vm2[i])
            sop(lambda s, qt=qt: s.activation(q_f16[:, sl], qt[:, sl], AF.Exp, bias=c_ap(3), scale=1.0e30))
            sm3[i] = sC.n

        def V3(i):
            qt = QT[i % 2]
            d0 = items[i][1]
            wait("vector", sC, sm3[i])
            vop(lambda v, qt=qt, d0=d0: v.scalar_tensor_tensor(
                hq8[:, d0:d0 + QB], q_f16[:, sl], 1.0, qt[:, sl], A.add, A.mult))
            vm3[i] = vC.n

        pre_emit(0)
        S1(0)
        for i in range(NBat + 1):
            if 0 <= i - 1 < NBat:
                V2(i - 1)
            if i < NBat:
                V1(i)
            if 0 <= i - 1 < NBat:
                S3(i - 1)
                V3(i - 1)
            if i + 1 < NBat:
                pre_emit(i + 1)
                S1(i + 1)
            if i < NBat:
                S2(i)
        return [vm2[i] for i in range(NBat)]

    def quant_chain_o_list(cols, pre_emit=None):
        NBat = len(cols)
        vm1, vm2, vm3, sm1, sm2, sm3 = {}, {}, {}, {}, {}, {}
        QL = (q_l, q_l2)
        QT = (q_t, q_t2)
        QA = (q_t16, q_t162)
        sl = slice(0, QB)
        entry_v = vC.n
        if pre_emit is None:
            pre_emit = lambda i: None

        def S1(i):
            ql, qa = QL[i % 2], QA[i % 2]
            s0 = cols[i]
            wait("scalar", vC, vm2.get(i - 2, entry_v))
            sop(lambda s, qa=qa, s0=s0: s.activation(qa[:, sl], qo[:, s0:s0 + QB], AF.Abs, bias=0.0, scale=1.0))
            sop(lambda s, ql=ql, qa=qa: s.activation(ql[:, sl], qa[:, sl], AF.Ln, bias=c_ap(0), scale=1.0))
            sm1[i] = sC.n

        def V1(i):
            ql = QL[i % 2]
            wait("vector", sC, max(sm1[i], sm2.get(i - 1, 0)))
            vop(lambda v, ql=ql: v.tensor_scalar(q_w0[:, sl], ql[:, sl], 1.0 / LN2_, 14.0, A.mult, A.add))
            vop(lambda v: v.tensor_scalar(q_w0[:, sl], q_w0[:, sl], 0.0, 30.0, A.max, A.min))
            lvl2(q_w0, q_w, q_c, 16, QB)
            for k in (8, 4, 2, 1):
                lvl(q_w, q_c, k, QB)
            vop(lambda v: v.scalar_tensor_tensor(q_ec[:, sl], q_w0[:, sl], -24.0, q_w[:, sl], A.add, A.subtract))
            vm1[i] = vC.n

        def S2(i):
            wait("scalar", vC, vm1[i])
            einc("scalar", sC, lambda s: s.activation(q_inv[:, sl], q_ec[:, sl], AF.Exp, bias=0.0, scale=-LN2_))
            sop(lambda s: s.activation(q_scl[:, sl], q_ec[:, sl], AF.Exp, bias=0.0, scale=LN2_))
            sm2[i] = sC.n

        def V2(i):
            ql, qt = QL[i % 2], QT[i % 2]
            s0 = cols[i]
            wait("vector", sC, sm2[i])
            vop(lambda v, qt=qt, s0=s0: v.tensor_tensor(qt[:, sl], qo[:, s0:s0 + QB], q_inv[:, sl], A.mult))
            vop(lambda v, ql=ql, qt=qt: v.tensor_scalar(ql[:, sl], qt[:, sl], 0.5, 1024.25 + 1e-7, A.mult, A.add))
            lvl2(ql, q_d, q_c, 2048, QB)
            lvl(q_d, q_c, 1024, QB)
            vop(lambda v: v.tensor_scalar(q_c[:, sl], q_d[:, sl], 512.0, -512.0, A.is_ge, A.mult))
            vop(lambda v: v.tensor_tensor(q_w[:, sl], q_d[:, sl], q_c[:, sl], A.add))
            for k in (256, 128, 64, 32, 16, 8, 4, 2, 1):
                lvl(q_w, q_c, k, QB)
            vop(lambda v: v.tensor_scalar(q_c[:, sl], q_w[:, sl], 0.5 + 3e-7, None, A.is_ge))
            vop(lambda v, ql=ql: v.tensor_tensor(q_d[:, sl], ql[:, sl], q_w[:, sl], A.subtract))
            vop(lambda v: v.scalar_tensor_tensor(q_d[:, sl], q_d[:, sl], 2.0, q_c[:, sl], A.mult, A.add))
            vop(lambda v, qt=qt: v.scalar_tensor_tensor(qt[:, sl], q_d[:, sl], -2048.0, q_scl[:, sl], A.add, A.mult))
            vm2[i] = vC.n

        def S3(i):
            qt, qa = QT[i % 2], QA[i % 2]
            wait("scalar", vC, vm2[i])
            sop(lambda s, qa=qa, qt=qt: s.activation(qa[:, sl], qt[:, sl], AF.Abs, bias=0.0, scale=1.0))
            sop(lambda s, qa=qa: s.activation(q_f16[:, sl], qa[:, sl], AF.Exp, bias=c_ap(6), scale=1.0e30))
            sm3[i] = sC.n

        def V3(i):
            qt = QT[i % 2]
            d0 = cols[i]
            wait("vector", sC, sm3[i])
            vop(lambda v, qt=qt, d0=d0: v.scalar_tensor_tensor(
                qo[:, d0:d0 + QB], q_f16[:, sl], 1.0, qt[:, sl], A.add, A.mult))
            vm3[i] = vC.n

        pre_emit(0)
        S1(0)
        for i in range(NBat + 1):
            if 0 <= i - 1 < NBat:
                V2(i - 1)
            if i < NBat:
                V1(i)
            if 0 <= i - 1 < NBat:
                S3(i - 1)
                V3(i - 1)
            if i + 1 < NBat:
                pre_emit(i + 1)
                S1(i + 1)
            if i < NBat:
                S2(i)

    gCast = gpP.group()
    einc("gpsimd", gCast, lambda g: g.dma_start(out=qkvscr[:, :], in_=wqkv_h[:, :]))
    einc("gpsimd", gCast, lambda g: g.dma_start(out=projscr[:, :], in_=wproj_h[:, :]))
    einc("gpsimd", gCast, lambda g: g.dma_start(out=w1scr[:, :], in_=w1_h[:, :]))
    D_CAST = gCast.n
    gSmall = gpP.group()
    einc("gpsimd", gSmall, lambda g: g.dma_start(
        out=b1sb[:, :], in_=b1_h.rearrange("(a b) -> b a", b=128),
        allow_slow_non_contiguous=True))
    einc("gpsimd", gSmall, lambda g: g.dma_start(
        out=bprojrep[:, :], in_=bproj_h.unsqueeze(0).broadcast_to([128, C]),
        allow_slow_non_contiguous=True))
    einc("gpsimd", gSmall, lambda g: g.dma_start(
        out=bqrep[:, :], in_=b2_h.unsqueeze(0).broadcast_to([128, C]),
        allow_slow_non_contiguous=True))
    einc("gpsimd", gSmall, lambda g: g.dma_start(out=identT[:, :], in_=ident_h[:, :]))
    D_SMALL = gSmall.n

    gWtr = syP.group()
    wait("sync", gCast, D_CAST)
    for j in range(8):
        einc("sync", gWtr, lambda s, j=j: s.dma_start_transpose(
            wqT[j][:, :], qkvscr[0:C, j * 128:(j + 1) * 128]))
        einc("sync", gWtr, lambda s, j=j: s.dma_start_transpose(
            wkvT[j][:, :], qkvscr[C:3 * C, j * 128:(j + 1) * 128]))
        einc("sync", gWtr, lambda s, j=j: s.dma_start_transpose(
            wpT[j][:, :], projscr[0:C, j * 128:(j + 1) * 128]))
    D_WTR = gWtr.n
    gW1 = syP.group()
    for j in range(8):
        einc("sync", gW1, lambda s, j=j: s.dma_start_transpose(
            qsT[:, :], w1scr[0:HID // 2, j * 128:(j + 1) * 128]))
        einc("sync", gW1, lambda s, j=j: s.dma_start_transpose(
            qsT2[:, :], w1scr[HID // 2:HID, j * 128:(j + 1) * 128]))
        wait("sync", gW1, gW1.n)
        einc("sync", gW1, lambda s, j=j: s.dma_start(
            out=w1T_scr[j * 128:(j + 1) * 128, 0:HID // 2], in_=qsT[:, :]))
        einc("sync", gW1, lambda s, j=j: s.dma_start(
            out=w1T_scr[j * 128:(j + 1) * 128, HID // 2:HID], in_=qsT2[:, :]))
        wait("sync", gW1, gW1.n)
    D_W1T = gW1.n

    def ln_stats_apply(col, src_ap, dst_ap, cache):
        wait("vector", vC, vC.n)
        einc("vector", vC, lambda v: v.bn_stats(bnbuf[:, 0:6], src_ap[:, 0:512]))
        einc("vector", vC, lambda v: v.bn_stats(bnbuf[:, 6:12], src_ap[:, 512:1024]))
        wait("vector", vC, vC.n)
        einc("vector", vC, lambda v: v.bn_aggr(st4[:, 0:2], bnbuf[:, :]))
        wait("scalar", vC, vC.n)
        einc("scalar", sC, lambda s: s.activation(st4[:, 2:3], st4[:, 1:2], AF.Sqrt, bias=c_ap(7), scale=1.0))
        wait("vector", sC, sC.n)
        if cache:
            rap, bap = statR[:, col:col + 1], statB[:, col:col + 1]
        else:
            rap, bap = st4[:, 3:4], st4[:, 4:5]
        einc("vector", vC, lambda v: v.reciprocal(rap, st4[:, 2:3]))
        wait("vector", vC, vC.n)
        einc("vector", vC, lambda v: v.scalar_tensor_tensor(bap, st4[:, 0:1], -1.0, rap, A.mult, A.mult))
        wait("scalar", vC, vC.n)
        einc("scalar", sC, lambda s: s.activation(dst_ap, src_ap, AF.Identity, bias=bap, scale=rap))

    def ln_apply_cached(col, src_ap, dst_ap):
        einc("scalar", sC, lambda s: s.activation(
            dst_ap, src_ap, AF.Identity,
            bias=statB[:, col:col + 1], scale=statR[:, col:col + 1]))

    lnT_tiles = [(lnT, 0), (lnT, CH), (lnT, 2 * CH), (lnT, 3 * CH),
                 (lnT2, 0), (lnT2, CH), (lnT2, 2 * CH), (lnT2, 3 * CH)]
    qsT_tiles = [(qsT, 0), (qsT, CH), (qsT, 2 * CH), (qsT, 3 * CH),
                 (qsT2, 0), (qsT2, CH), (qsT2, 2 * CH), (qsT2, 3 * CH)]

    def lnT_ap(j, a, b):
        tns, coff = lnT_tiles[j]
        return tns[:, coff + a: coff + b]

    def qsT_ap(j, a, b, r0=0, r1=128):
        tns, coff = qsT_tiles[j]
        return tns[r0:r1, coff + a: coff + b]

    tr_ev = {"prev": 0}
    def transpose_chunk(tiles, src):
        s_pre = sC.n
        t_last = 0
        for j in range(8):
            tns, coff = tiles[j]
            wait("tensor", sC, max(tr_ev["prev"], s_pre))
            for subq in range(4):
                ins = lambda t, subq=subq, j=j, src=src: t.matmul(
                    p_tr[:, subq * 128:(subq + 1) * 128],
                    src[:, subq * C + j * 128: subq * C + (j + 1) * 128],
                    identT[:, :], start=(subq == 0), stop=(subq == 3))
                if subq == 3:
                    einc("tensor", tC, lambda t, ins=ins: ins(t))
                else:
                    emit("tensor", ins)
            t_last = tC.n
            wait("scalar", tC, t_last)
            einc("scalar", sC, lambda s, tns=tns, coff=coff: s.activation(
                tns[:, coff:coff + 512], p_tr[:, :], AF.Identity, bias=0.0, scale=1.0))
            tr_ev["prev"] = sC.n
        return (t_last, sC.n)

    wait("scalar", gW1, D_W1T)
    wait("vector", gW1, D_W1T)
    gW2L = gpP.group()
    gW2S = gpP.group()

    def emit_w2_tile(i):
        d_tile = gW2L.n + 16
        einc("gpsimd", gW2L, lambda g, i=i: g.dma_start(
            out=w2f[:, :], in_=w2_h[i * 128:(i + 1) * 128, :]))
        wait("scalar", gW2L, d_tile)
        wait("vector", gW2L, d_tile)
        wait("vector", gW2S, gW2S.n)
        for hh in range(HID // QB):
            sl = slice(hh * QB, (hh + 1) * QB)
            wq_dst = qsT[:, (hh % 2) * QB:(hh % 2 + 1) * QB] if hh < 2 else \
                qsT2[:, (hh % 2) * QB:(hh % 2 + 1) * QB]
            quant_chain(QB, w2f[:, sl], wq_dst, "w")
        v_tile = vC.n
        wait("gpsimd", vC, v_tile)
        einc("gpsimd", gW2S, lambda g, i=i: g.dma_start(
            out=wqscr[i * 128:(i + 1) * 128, 0:HID // 2], in_=qsT[:, :]))
        einc("gpsimd", gW2S, lambda g, i=i: g.dma_start(
            out=wqscr[i * 128:(i + 1) * 128, HID // 2:HID], in_=qsT2[:, :]))

    s_ln1_prev = 0
    t_trA_prev, s_trA = 0, 0
    ev_done_t = 0
    for chk in range(NCH):
        gX = syP.group()
        d_x = gX.n + 64
        wait("sync", sC, s_ln1_prev)
        for subq in range(4):
            einc("sync", gX, lambda s, chk=chk, subq=subq: s.dma_start(
                out=xc[:, subq * C:(subq + 1) * C],
                in_=x_h[chk * CH + subq * 128: chk * CH + (subq + 1) * 128, :]))
        wait("vector", gX, d_x)
        wait("scalar", gX, d_x)
        wait("scalar", tC, t_trA_prev)
        wait("tensor", gSmall, D_SMALL if chk == 0 else 0)
        for sub in range(4):
            ln_stats_apply(chk * 4 + sub, xc[:, sub * C:(sub + 1) * C],
                           lnc[:, sub * C:(sub + 1) * C], cache=True)
        s_ln1_prev = sC.n
        wait("scalar", sC, s_ln1_prev)
        t_trA_prev, s_trA = transpose_chunk(lnT_tiles, lnc)
        for sub in range(4):
            wait("tensor", sC, s_trA)
            wait("tensor", gWtr, D_WTR)
            for c in range(8):
                for f in range(4):
                    ins = lambda t, c=c, f=f, sub=sub: t.matmul(
                        p_kv[:, f * 512:(f + 1) * 512],
                        lnT_ap(c, sub * 128, (sub + 1) * 128),
                        wkvT[c][:, f * 512:(f + 1) * 512],
                        start=(c == 0), stop=(c == 7))
                    if c == 7 and f == 3:
                        einc("tensor", tC, lambda t, ins=ins: ins(t))
                    else:
                        emit("tensor", ins)
            t_kv = tC.n
            wait("scalar", tC, max(t_kv, ev_done_t))
            einc("scalar", sC, lambda s: s.activation(ek[:, :], p_kv[:, 0:C], AF.Exp, bias=0.0, scale=1.0))
            s_ek = sC.n
            einc("scalar", sC, lambda s: s.activation(
                vv[:, :], p_kv[:, C:2 * C], AF.Identity, bias=0.0, scale=1.0))
            s_vv = sC.n
            wait("tensor", sC, s_vv)
            first = (chk == 0) and (sub == 0)
            last = (chk == NCH - 1) and (sub == 3)
            for g in range(8):
                emit("tensor", lambda t, g=g, first=first, last=last: t.matmul(
                    p_ctx[:, g * 128:(g + 1) * 128], ek[:, g * 128:(g + 1) * 128],
                    vv[:, g * 128:(g + 1) * 128],
                    start=(first and g % 4 == 0), stop=(last and g % 4 == 3)))
                ins2 = lambda t, g=g, first=first, last=last: t.matmul(
                    p_den[:, g:g + 1], ek[:, g * 128:(g + 1) * 128],
                    ones_col[:, 0:1], start=(first and g == 0), stop=(last and g == 7))
                if g == 7:
                    einc("tensor", tC, lambda t, ins2=ins2: ins2(t))
                else:
                    emit("tensor", ins2)
            ev_done_t = tC.n
        emit_w2_tile(chk)
    T_PHA = tC.n

    wait("vector", tC, T_PHA)
    einc("vector", vC, lambda v: v.reciprocal(rdq[:, 0:8], p_den[:, 0:8]))
    v_rd = vC.n
    V_ENDA = vC.n
    wait("scalar", vC, v_rd)
    for g in range(8):
        einc("scalar", sC, lambda s, g=g: s.activation(
            ctxn[g][:, :], p_ctx[:, g * 128:(g + 1) * 128], AF.Identity,
            bias=0.0, scale=rdq[:, g:g + 1]))
    S_CTXN = sC.n

    D_WQ = gW2S.n
    gWq = syP.group()
    gWq8 = gpP.group()
    wait("sync", gW2S, D_WQ)
    for j in range(32):
        einc("sync", gWq, lambda s, j=j: s.dma_start_transpose(
            qsT[:, 0:C], wqscr[0:C, j * 128:(j + 1) * 128]))
        wait("vector", gWq, gWq.n)
        wait("vector", gWq8, gWq8.n)
        einc("vector", vC, lambda v: v.tensor_copy(wq8stg[:, :], qsT[:, 0:C]))
        wait("sync", vC, vC.n)
        wait("gpsimd", vC, vC.n)
        einc("gpsimd", gWq8, lambda g, j=j: g.dma_start(
            out=wqT8h[j * 128:(j + 1) * 128, :], in_=wq8stg[:, :]))
    V_WQT = vC.n
    D_WQ8 = gWq8.n

    wait("scalar", gSmall, D_SMALL)
    wait("vector", gSmall, D_SMALL)
    quant_chain(C, bqrep[:, :], bqrep[:, :], "b")
    V_BQ = vC.n
    S_PREP = sC.n

    wait("scalar", gSmall, D_SMALL)
    wait("scalar", gWq, gWq.n)
    wait("scalar", gW2S, D_WQ)
    wait("scalar", vC, V_WQT)
    t_wq_free = T_PHA
    c1 = 1.0 / LN2_
    v_prev_fin = 0
    t_tr2_prev = 0
    t_qsT_prev, s_qsT_prev = 0, 0
    g_out_prev = None
    t_m2_prev = T_PHA
    t_pj_prev = 0
    t_m1_done = [T_PHA, T_PHA]
    h_prev_marks = None

    def prefetch_xb(chk, t_gate):
        gXB = gpP.group()
        wait("gpsimd", tC, t_gate)
        for subq in range(4):
            tns = lnT if subq < 2 else lnT2
            c0 = (subq % 2) * C
            einc("gpsimd", gXB, lambda g, chk=chk, subq=subq, tns=tns, c0=c0: g.dma_start(
                out=tns[:, c0:c0 + C],
                in_=x_h[chk * CH + subq * 128: chk * CH + (subq + 1) * 128, :]))
        return gXB, gXB.n
    d_out_prev = 0
    gXB, d_xb = prefetch_xb(0, T_PHA)
    for chk in range(NCH):
        gX = syP.group()
        d_x = gX.n + 64
        wait("sync", vC, v_prev_fin)
        wait("sync", sC, S_CTXN if chk == 0 else 0)
        for subq in range(4):
            einc("sync", gX, lambda s, chk=chk, subq=subq: s.dma_start(
                out=xc[:, subq * C:(subq + 1) * C],
                in_=x_h[chk * CH + subq * 128: chk * CH + (subq + 1) * 128, :]))
        wait("scalar", gXB, d_xb)
        wait("scalar", vC, V_ENDA)
        wait("scalar", tC, t_tr2_prev)
        for sub in range(4):
            xtns = lnT if sub < 2 else lnT2
            ln_apply_cached(chk * 4 + sub, xtns[:, (sub % 2) * C:(sub % 2 + 1) * C],
                            lnc[:, sub * C:(sub + 1) * C])
        s_ln1b = sC.n
        wait("scalar", sC, s_ln1b)
        t_tr1, s_tr1 = transpose_chunk(lnT_tiles, lnc)
        s_eq = 0
        for sub in range(4):
            wait("tensor", sC, max(S_CTXN, s_eq, s_tr1))
            wait("tensor", vC, max(V_WQT, V_BQ))
            for c in range(8):
                for f in range(2):
                    ins = lambda t, c=c, f=f, sub=sub: t.matmul(
                        p_q[:, f * 512:(f + 1) * 512],
                        lnT_ap(c, sub * 128, (sub + 1) * 128),
                        wqT[c][:, f * 512:(f + 1) * 512],
                        start=(c == 0), stop=(c == 7))
                    if c == 7 and f == 1:
                        einc("tensor", tC, lambda t, ins=ins: ins(t))
                    else:
                        emit("tensor", ins)
            t_q = tC.n
            wait("scalar", tC, t_q)
            wait("scalar", tC, t_qsT_prev)
            einc("scalar", sC, lambda s, sub=sub: s.activation(
                eq_[:, sub * C:(sub + 1) * C], p_q[:, :], AF.Exp, bias=0.0, scale=1.0))
            s_eq = sC.n
            wait("vector", sC, s_eq)
            einc("vector", vC, lambda v, sub=sub: v.tensor_reduce(
                dq[:, :].unsqueeze(2),
                eq_[:, sub * C:(sub + 1) * C].rearrange("p (h d) -> p h d", h=NH),
                AX.X, A.add))
            wait("vector", vC, vC.n)
            einc("vector", vC, lambda v: v.reciprocal(rdq[:, 0:16], dq[:, :]))
            wait("vector", vC, vC.n)
            for hh in range(NH):
                ins = lambda v, hh=hh, sub=sub: v.tensor_scalar(
                    eq_[:, sub * C + hh * HD: sub * C + (hh + 1) * HD],
                    eq_[:, sub * C + hh * HD: sub * C + (hh + 1) * HD],
                    rdq[:, hh:hh + 1], None, A.mult)
                if hh == NH - 1:
                    einc("vector", vC, lambda v, ins=ins: ins(v))
                else:
                    emit("vector", ins)
            v_qs = vC.n
            wait("scalar", tC, t_pj_prev)
            for jg in range(2):
                wait("tensor", vC, v_qs)
                wait("tensor", sC, tr_ev["prev"])
                for q in range(4):
                    j = jg * 4 + q
                    ins = lambda t, q=q, j=j, sub=sub: t.matmul(
                        p_tr[:, q * 128:(q + 1) * 128],
                        eq_[:, sub * C + j * 128: sub * C + (j + 1) * 128],
                        identT[:, :], start=(q == 0), stop=(q == 3))
                    if q == 3:
                        einc("tensor", tC, lambda t, ins=ins: ins(t))
                    else:
                        emit("tensor", ins)
                wait("scalar", tC, tC.n)
                for q in range(4):
                    tns, coff = qsT_tiles[jg * 4 + q]
                    einc("scalar", sC, lambda s, tns=tns, coff=coff, q=q, sub=sub: s.activation(
                        tns[:, coff + sub * 128: coff + (sub + 1) * 128],
                        p_tr[:, q * 128:(q + 1) * 128], AF.Identity, bias=0.0, scale=1.0))
                tr_ev["prev"] = sC.n
        t_qsT_prev, s_qsT_prev = tC.n, sC.n
        wait("tensor", sC, s_qsT_prev)
        for hg in range(4):
            for hh in range(4):
                h = hg * 4 + hh
                g = h // 2
                r0 = (h % 2) * 64
                ins = lambda t, g=g, r0=r0, hh=hh: t.matmul(
                    p_m1[0:64, hh * CH:(hh + 1) * CH], ctxn[g][r0:r0 + 64, r0:r0 + 64],
                    qsT_ap(g, 0, CH, r0, r0 + 64), start=True, stop=True)
                if hh == 3:
                    einc("tensor", tC, lambda t, ins=ins: ins(t))
                else:
                    emit("tensor", ins)
            t_at = tC.n
            wait("vector", tC, t_at)
            for hh in range(4):
                h = hg * 4 + hh
                g = h // 2
                r0 = (h % 2) * 64
                tns, coff = qsT_tiles[g]
                ins = lambda v, tns=tns, coff=coff, r0=r0, hh=hh: v.tensor_copy(
                    tns[r0:r0 + 64, coff:coff + CH], p_m1[0:64, hh * CH:(hh + 1) * CH])
                if hh == 3:
                    einc("vector", vC, lambda v, ins=ins: ins(v))
                else:
                    emit("vector", ins)
            wait("tensor", vC, vC.n)
        v_attnT = vC.n
        for sub in range(4):
            wait("tensor", vC, v_attnT)
            for c in range(8):
                for f in range(2):
                    ins = lambda t, c=c, f=f, sub=sub: t.matmul(
                        p_q[:, f * 512:(f + 1) * 512],
                        qsT_ap(c, sub * 128, (sub + 1) * 128),
                        wpT[c][:, f * 512:(f + 1) * 512],
                        start=(c == 0), stop=(c == 7))
                    if c == 7 and f == 1:
                        einc("tensor", tC, lambda t, ins=ins: ins(t))
                    else:
                        emit("tensor", ins)
            t_pj = tC.n
            wait("vector", tC, t_pj)
            wait("vector", gX, d_x)
            wait("vector", gSmall, D_SMALL)
            if g_out_prev is not None:
                wait("vector", g_out_prev, d_out_prev)
            einc("vector", vC, lambda v, sub=sub: v.tensor_tensor(
                yc[:, sub * C:(sub + 1) * C], p_q[:, :], bprojrep[:, :], A.add))
            wait("vector", vC, vC.n)
            einc("vector", vC, lambda v, sub=sub: v.tensor_tensor(
                yc[:, sub * C:(sub + 1) * C], yc[:, sub * C:(sub + 1) * C],
                xc[:, sub * C:(sub + 1) * C], A.add))
            wait("tensor", vC, vC.n)
            ln_stats_apply(0, yc[:, sub * C:(sub + 1) * C],
                           lnc[:, sub * C:(sub + 1) * C], cache=False)
        s_ln2 = sC.n
        t_pj_prev = tC.n
        t_tr2, s_tr2 = transpose_chunk(lnT_tiles, lnc)
        t_tr2_prev = t_tr2
        wait("scalar", tC, t_m2_prev)
        wait("vector", tC, t_m2_prev)
        def emit_hb_block(half, j, prev_marks):
            hb = 4 * half + j
            buf = hb % 2
            gW = gpP.group()
            d_w1 = gW.n + 128
            wait("gpsimd", tC, t_m1_done[buf])
            for cq in range(8):
                einc("gpsimd", gW, lambda g, hb=hb, buf=buf, cq=cq: g.dma_start(
                    out=w1buf[:, buf * HID + cq * 512: buf * HID + (cq + 1) * 512],
                    in_=w1T_scr[cq * 128:(cq + 1) * 128, hb * 512:(hb + 1) * 512]))
            wait("tensor", gW, d_w1)
            wait("tensor", sC, s_tr2)
            for c in range(8):
                for k in range(4):
                    ins = lambda t, c=c, k=k, buf=buf: t.matmul(
                        p_m1[:, k * 512:(k + 1) * 512],
                        w1buf[:, buf * HID + c * 512 + k * 128: buf * HID + c * 512 + (k + 1) * 128],
                        lnT_ap(c, 0, CH), start=(c == 0), stop=(c == 7))
                    if c == 7 and k == 3:
                        einc("tensor", tC, lambda t, ins=ins: ins(t))
                    else:
                        emit("tensor", ins)
            t_m1_done[buf] = tC.n
            wait("scalar", tC, t_m1_done[buf])
            if prev_marks is not None:
                wait("scalar", vC, prev_marks[2 * j + 1])
            for k in range(4):
                ht = hb * 4 + k
                ins = lambda s, k=k, ht=ht, half=half: s.activation(
                    hbuf[:, (ht - 16 * half) * CH:(ht - 16 * half + 1) * CH],
                    p_m1[:, k * 512:(k + 1) * 512],
                    AF.Relu, bias=b1sb[:, ht:ht + 1], scale=1.0)
                if k == 3:
                    einc("scalar", sC, lambda s, ins=ins: ins(s))
                else:
                    emit("scalar", ins)
            wait("tensor", sC, sC.n)

        for half in range(2):
            done_blocks = set()

            def mk_pre(half, prev_marks, done_blocks):
                def pre(i):
                    j = i // 2
                    if j in done_blocks or j > 3:
                        return
                    done_blocks.add(j)
                    emit_hb_block(half, j, prev_marks)
                    wait("scalar", sC, sC.n)
                return pre

            h_prev_marks = quant_chain_h_list(
                [(bb * QB, half * 8192 + bb * QB) for bb in range(8)],
                pre_emit=mk_pre(half, h_prev_marks, done_blocks))
        v_hq8 = vC.n
        if chk + 1 < NCH:
            gXB, d_xb = prefetch_xb(chk + 1, max(t_m1_done[0], t_m1_done[1]))
        wq_hold = {"free": t_wq_free}

        def emit_sp(sp):
            wait("tensor", vC, v_hq8)
            t_hc = {}
            for hc in range(32):
                gate = wq_hold["free"] if hc < 4 else t_hc[hc - 4]
                wait("gpsimd", tC, gate)
                slot = wqS[hc % 4]
                einc("gpsimd", slot, lambda g, hc=hc: g.dma_start(
                    out=wq8buf[:, (hc % 4) * C:(hc % 4 + 1) * C],
                    in_=wqT8h[hc * 128:(hc + 1) * 128, :]))
                d_hc = slot.n
                wait("tensor", slot, d_hc)
                for sub in (2 * sp, 2 * sp + 1):
                    pq = p_q if sub % 2 == 0 else p_kv
                    for f in range(2):
                        ins = lambda t, hc=hc, f=f, sub=sub, pq=pq: t.matmul(
                            pq[:, f * 512:(f + 1) * 512],
                            hq8[:, hc * CH + sub * 128: hc * CH + (sub + 1) * 128],
                            wq8buf[:, (hc % 4) * C + f * 512: (hc % 4) * C + (f + 1) * 512],
                            start=(hc == 0), stop=(hc == 31))
                        if f == 1 and sub % 2 == 1:
                            einc("tensor", tC, lambda t, ins=ins: ins(t))
                        else:
                            emit("tensor", ins)
                t_hc[hc] = tC.n
            wq_hold["free"] = tC.n
            t_m2 = tC.n
            wait("vector", tC, t_m2)
            for sub in (2 * sp, 2 * sp + 1):
                pq = p_q if sub % 2 == 0 else p_kv
                einc("vector", vC, lambda v, sub=sub, pq=pq: v.tensor_tensor(
                    qo[:, sub * C:(sub + 1) * C], pq[:, 0:C], bqrep[:, :], A.add))
            wait("tensor", vC, vC.n)

        sp_done = set()

        def pre_o(i):
            sp = i // 2
            if sp in sp_done or sp > 1:
                return
            sp_done.add(sp)
            emit_sp(sp)
            wait("scalar", vC, vC.n)

        quant_chain_o_list([bb * QB for bb in range(4 * C // QB)], pre_emit=pre_o)
        t_wq_free = wq_hold["free"]
        t_m2_prev = tC.n
        for sub in range(4):
            ins = lambda v, sub=sub: v.tensor_tensor(
                yc[:, sub * C:(sub + 1) * C], yc[:, sub * C:(sub + 1) * C],
                qo[:, sub * C:(sub + 1) * C], A.add)
            einc("vector", vC, lambda v, ins=ins: ins(v))
        v_prev_fin = vC.n
        wait("sync", vC, v_prev_fin)
        gO = syP.group()
        for subq in range(4):
            einc("sync", gO, lambda s, chk=chk, subq=subq: s.dma_start(
                out=out_h[chk * CH + subq * 128: chk * CH + (subq + 1) * 128, :],
                in_=yc[:, subq * C:(subq + 1) * C]))
        g_out_prev, d_out_prev = gO, gO.n

    for c in syP.ctrs + scP.ctrs + gpP.ctrs + wqS:
        if c.n > 0:
            wait("sync", c, c.n)

    with nc.Block() as block:
        @block.sync
        def _(e):
            for fn in prog["sync"]:
                fn(e)

        @block.scalar
        def _(e):
            for fn in prog["scalar"]:
                fn(e)

        @block.vector
        def _(e):
            for fn in prog["vector"]:
                fn(e)

        @block.tensor
        def _(e):
            for fn in prog["tensor"]:
                fn(e)

        @block.gpsimd
        def _(e):
            for fn in prog["gpsimd"]:
                fn(e)

    return nc


def kernel(L0_attn_x, L0_attn_w_qkv, L0_attn_w_proj, L0_attn_b_proj,
           L0_ffn_w1, L0_ffn_b1, L0_ffn_w2, L0_ffn_b2):
    x = np.ascontiguousarray(np.asarray(L0_attn_x, dtype=np.float32))
    w_qkv = np.ascontiguousarray(np.asarray(L0_attn_w_qkv, dtype=np.float32))
    w_proj = np.ascontiguousarray(np.asarray(L0_attn_w_proj, dtype=np.float32))
    b_proj = np.ascontiguousarray(np.asarray(L0_attn_b_proj, dtype=np.float32))
    w1 = np.ascontiguousarray(np.asarray(L0_ffn_w1, dtype=np.float32))
    b1 = np.ascontiguousarray(np.asarray(L0_ffn_b1, dtype=np.float32))
    w2 = np.ascontiguousarray(np.asarray(L0_ffn_w2, dtype=np.float32))
    b2 = np.ascontiguousarray(np.asarray(L0_ffn_b2, dtype=np.float32))

    nc = bass.Bass("TRN2", target_bir_lowering=False)
    with ExitStack() as es:
        build(nc, es)

    ident = np.eye(128, dtype=np.float32).astype(ml_dtypes.bfloat16)
    in_maps = [
        {"x": x[b], "w_qkv": w_qkv, "w_proj": w_proj, "b_proj": b_proj,
         "w1": w1, "b1": b1, "w2": w2, "b2": b2, "ident": ident}
        for b in range(8)
    ]
    res = run_bass_kernel_spmd(nc, in_maps, core_ids=list(range(8)))
    out = np.stack([np.asarray(r["out"], dtype=np.float32) for r in res.results],
                   axis=0)
    return out


# version 18 (final)
# speedup vs baseline: 1.0130x; 1.0130x over previous
import math
import ml_dtypes
import numpy as np
from contextlib import ExitStack

import concourse.bass as bass
import concourse.mybir as mybir
from concourse.bass_utils import run_bass_kernel_spmd

F32 = mybir.dt.float32
BF16 = mybir.dt.bfloat16
FP8 = mybir.dt.float8e4
I32 = mybir.dt.int32
A = mybir.AluOpType
AF = mybir.ActivationFunctionType
AX = mybir.AxisListType

T = 4096
C = 1024
HID = 4096
NH = 16
HD = 64
CH = 512
NCH = T // CH
LN2_ = math.log(2.0)
EPS = 1e-5
QB = 1024


class Ctr:
    def __init__(self, sem, step=1):
        self.sem = sem
        self.step = step
        self.n = 0

    def inc(self, instr):
        instr.then_inc(self.sem, self.step)
        self.n += self.step
        return self.n


def build(nc: bass.Bass, es: ExitStack):
    dram = lambda n, sh, dt, k: nc.dram_tensor(n, sh, dt, kind=k).ap()
    x_h = dram("x", [T, C], F32, "ExternalInput")
    wqkv_h = dram("w_qkv", [3 * C, C], F32, "ExternalInput")
    wproj_h = dram("w_proj", [C, C], F32, "ExternalInput")
    bproj_h = dram("b_proj", [C], F32, "ExternalInput")
    w1_h = dram("w1", [HID, C], F32, "ExternalInput")
    b1_h = dram("b1", [HID], F32, "ExternalInput")
    w2_h = dram("w2", [C, HID], F32, "ExternalInput")
    b2_h = dram("b2", [C], F32, "ExternalInput")
    ident_h = dram("ident", [128, 128], BF16, "ExternalInput")
    out_h = dram("out", [T, C], F32, "ExternalOutput")
    qkvscr = dram("qkvscr", [3 * C, C], BF16, "Internal")
    projscr = dram("projscr", [C, C], BF16, "Internal")
    w1scr = dram("w1scr", [HID, C], BF16, "Internal")
    w1T_scr = dram("w1T_scr", [C, HID], BF16, "Internal")
    wqscr = dram("wqscr", [C, HID], BF16, "Internal")
    wqT8h = dram("wqT8h", [HID, C], FP8, "Internal")

    sb = lambda n, sh, dt: es.enter_context(nc.sbuf_tensor(n, sh, dt))

    wqT = [sb(f"wqT{j}", [128, C], BF16) for j in range(8)]
    wpT = [sb(f"wpT{j}", [128, C], BF16) for j in range(8)]
    wq8buf = sb("wq8buf", [128, 4 * C], FP8)
    wq8stg = sb("wq8stg", [128, C], FP8)
    ctxn = [sb(f"ctxn{g}", [128, 128], BF16) for g in range(8)]
    b1sb = sb("b1sb", [128, 32], F32)
    identT = sb("identT", [128, 128], BF16)
    bprojrep = sb("bprojrep", [128, C], BF16)
    bqrep = sb("bqrep", [128, C], F32)
    statR = sb("statR", [128, 32], F32)
    statB = sb("statB", [128, 32], F32)
    ones_col = sb("ones_col", [128, 8], BF16)
    con = sb("con", [128, 12], F32)

    xc = sb("xc", [128, 4 * C], F32)
    lnc = sb("lnc", [128, 4 * C], BF16)
    lnT = sb("lnT", [128, 4 * CH], BF16)
    lnT2 = sb("lnT2", [128, 4 * CH], BF16)
    eq_ = sb("eq", [128, 4 * C], BF16)
    qsT = sb("qsT", [128, 4 * CH], BF16)
    qsT2 = sb("qsT2", [128, 4 * CH], BF16)
    dq = sb("dq", [128, 16], F32)
    rdq = sb("rdq", [128, 16], F32)
    yc = sb("yc", [128, 4 * C], F32)
    st4 = sb("st4", [128, 8], F32)
    bnbuf = sb("bnbuf", [128, 12], F32)
    hbuf = sb("hbuf", [128, HID * CH // 256], BF16)
    hq8 = sb("hq8", [128, HID * CH // 128], FP8)
    w1buf = sb("w1buf", [128, 2 * HID], BF16)
    q_l = sb("q_l", [128, QB], F32)
    q_i0 = sb("q_i0", [128, QB], I32)
    q_ec = sb("q_ec", [128, QB], BF16)
    q_inv = sb("q_inv", [128, QB], BF16)
    q_scl = sb("q_scl", [128, QB], BF16)
    q_t = sb("q_t", [128, QB], F32)
    q_d = sb("q_d", [128, QB], F32)
    q_w = sb("q_w", [128, QB], BF16)
    q_w0 = sb("q_w0", [128, QB], BF16)
    q_c = sb("q_c", [128, QB], BF16)
    q_t16 = sb("q_t16", [128, QB], BF16)
    q_t162 = sb("q_t162", [128, QB], BF16)
    q_l2 = sb("q_l2", [128, QB], F32)
    q_t2 = sb("q_t2", [128, QB], F32)
    q_f16 = sb("q_f16", [128, QB], BF16)
    q_r = sb("q_r", [128, QB], BF16)
    ek = eq_[:, 0:C]
    vv = eq_[:, C:2 * C]
    qo = xc
    w2f = yc
    q_dm = q_d
    q_i5 = q_i0
    q_f = q_l
    q_u2 = q_l2
    q_a = q_t2
    wkvT = ([w1buf[:, j * 2 * C:(j + 1) * 2 * C] for j in range(4)]
            + [hbuf[:, j * 2 * C:(j + 1) * 2 * C] for j in range(4)])

    p_kv = es.enter_context(nc.psum_tensor("p_kv", [128, 2 * C], F32))
    p_ctx = es.enter_context(nc.psum_tensor("p_ctx", [128, C], F32))
    p_den = es.enter_context(nc.psum_tensor("p_den", [128, 512], F32))
    p_tr = es.enter_context(nc.psum_tensor("p_tr", [128, 512], F32))
    p_m1 = p_kv
    p_q = p_ctx

    sem = lambda n: es.enter_context(nc.semaphore(n))
    vC = Ctr(sem("s_v"))
    sC = Ctr(sem("s_s"))
    tC = Ctr(sem("s_t"))

    prog = {"sync": [], "scalar": [], "vector": [], "tensor": [], "gpsimd": []}

    def emit(eng, fn):
        prog[eng].append(fn)

    def wait(eng, ctr, val):
        if val > 0:
            emit(eng, lambda e, s=ctr.sem, v=val: e.wait_ge(s, v))

    def einc(eng, ctr, fn):
        ctr.n += ctr.step
        emit(eng, lambda e, fn=fn, s=ctr.sem, st=ctr.step: fn(e).then_inc(s, st))
        return ctr.n

    class DPool:

        def __init__(self, eng, k, tag):
            self.eng = eng
            self.ctrs = [Ctr(sem(f"{tag}{i}"), 16) for i in range(k)]
            self.idx = 0

        def group(self):
            c = self.ctrs[self.idx % len(self.ctrs)]
            self.idx += 1
            if c.n > 0:
                wait(self.eng, c, c.n)
            return c

    syP = DPool("sync", 8, "dy")
    scP = DPool("scalar", 8, "dc")
    gpP = DPool("gpsimd", 8, "dg")
    wqS = [Ctr(sem(f"wq{i}"), 16) for i in range(4)]

    def c_ap(j):
        return con[:, j:j + 1]

    for j, val in enumerate([1e-38, 9.0 * LN2_, -9.0 * LN2_, -248.0e30,
                             10.0 * LN2_, -10.0 * LN2_, -65520.0e30, EPS,
                             3.0 * LN2_, -3.0 * LN2_, 24.0 * LN2_, -24.0 * LN2_]):
        emit("vector", lambda v, j=j, val=val: v.memset(con[:, j:j + 1], val))
    einc("vector", vC, lambda v: v.memset(ones_col[:, :], 1.0))
    V_CONST = vC.n
    wait("scalar", vC, V_CONST)
    wait("tensor", vC, V_CONST)
    wait("gpsimd", vC, V_CONST)
    wait("sync", vC, V_CONST)

    def vop(fn):
        einc("vector", vC, fn)
        wait("vector", vC, vC.n)
        return vC.n

    def sop(fn):
        einc("scalar", sC, fn)
        wait("scalar", sC, sC.n)
        return sC.n

    def lvl(wbuf, cbuf, k, n):
        sl = slice(0, n)
        vop(lambda v, k=k: v.tensor_scalar(cbuf[:, sl], wbuf[:, sl], float(k), -float(k), A.is_ge, A.mult))
        vop(lambda v: v.tensor_tensor(wbuf[:, sl], wbuf[:, sl], cbuf[:, sl], A.add))

    def lvl2(win, wout, cbuf, k, n):
        sl = slice(0, n)
        vop(lambda v, k=k: v.tensor_scalar(cbuf[:, sl], win[:, sl], float(k), -float(k), A.is_ge, A.mult))
        vop(lambda v: v.tensor_tensor(wout[:, sl], win[:, sl], cbuf[:, sl], A.add))

    def quant_chain(n, src_ap, dst_ap, mode):
        sl = slice(0, n)
        wait("scalar", vC, vC.n)
        wait("vector", sC, sC.n)
        if mode == "h":
            sop(lambda s: s.activation(q_l[:, sl], src_ap, AF.Ln, bias=c_ap(0), scale=1.0))
            wait("vector", sC, sC.n)
            vop(lambda v: v.tensor_scalar(q_w[:, sl], q_l[:, sl], 1.0 / LN2_, 6.0, A.mult, A.add))
            vop(lambda v: v.tensor_scalar(q_w[:, sl], q_w[:, sl], 0.0, None, A.max))
            vop(lambda v: v.tensor_copy(q_w0[:, sl], q_w[:, sl]))
            for k in (8, 4, 2, 1):
                lvl(q_w, q_c, k, n)
            vop(lambda v: v.scalar_tensor_tensor(q_ec[:, sl], q_w0[:, sl], -9.0, q_w[:, sl], A.add, A.subtract))
            wait("scalar", vC, vC.n)
            einc("scalar", sC, lambda s: s.activation(q_inv[:, sl], q_ec[:, sl], AF.Exp, bias=0.0, scale=-LN2_))
            sop(lambda s: s.activation(q_scl[:, sl], q_ec[:, sl], AF.Exp, bias=0.0, scale=LN2_))
            wait("vector", sC, sC.n)
            vop(lambda v: v.tensor_tensor(q_t16[:, sl], src_ap, q_inv[:, sl], A.mult))
            vop(lambda v: v.tensor_scalar(q_w[:, sl], q_t16[:, sl], 0.5, 0.25 + 1e-13, A.mult, A.add))
            vop(lambda v: v.tensor_copy(q_w0[:, sl], q_w[:, sl]))
            for k in (8, 4, 2, 1):
                lvl(q_w, q_c, k, n)
            vop(lambda v: v.tensor_scalar(q_c[:, sl], q_w[:, sl], 0.5 + 1e-12, None, A.is_ge))
            vop(lambda v: v.tensor_tensor(q_r[:, sl], q_w0[:, sl], q_w[:, sl], A.subtract))
            vop(lambda v: v.scalar_tensor_tensor(q_r[:, sl], q_r[:, sl], 2.0, q_c[:, sl], A.mult, A.add))
            vop(lambda v: v.tensor_tensor(q_t16[:, sl], q_r[:, sl], q_scl[:, sl], A.mult))
            wait("scalar", vC, vC.n)
            sop(lambda s: s.activation(q_f16[:, sl], q_t16[:, sl], AF.Exp, bias=c_ap(3), scale=1.0e30))
            wait("vector", sC, sC.n)
            vop(lambda v: v.scalar_tensor_tensor(dst_ap, q_f16[:, sl], 1.0, q_t16[:, sl], A.add, A.mult))
            return
        halfeven = mode in ("w", "b")
        if mode == "o":
            sop(lambda s: s.activation(q_d[:, sl], src_ap, AF.Abs, bias=0.0, scale=1.0))
            sop(lambda s: s.activation(q_l[:, sl], q_d[:, sl], AF.Ln, bias=c_ap(0), scale=1.0))
            wait("vector", sC, sC.n)
            vop(lambda v: v.tensor_scalar(q_w[:, sl], q_l[:, sl], 1.0 / LN2_, 14.0, A.mult, A.add))
            vop(lambda v: v.tensor_scalar(q_w[:, sl], q_w[:, sl], 0.0, 30.0, A.max, A.min))
            vop(lambda v: v.tensor_copy(q_w0[:, sl], q_w[:, sl]))
            for k in (16, 8, 4, 2, 1):
                lvl(q_w, q_c, k, n)
            vop(lambda v: v.scalar_tensor_tensor(q_ec[:, sl], q_w0[:, sl], -24.0, q_w[:, sl], A.add, A.subtract))
            wait("scalar", vC, vC.n)
            einc("scalar", sC, lambda s: s.activation(q_inv[:, sl], q_ec[:, sl], AF.Exp, bias=0.0, scale=-LN2_))
            sop(lambda s: s.activation(q_scl[:, sl], q_ec[:, sl], AF.Exp, bias=0.0, scale=LN2_))
            wait("vector", sC, sC.n)
            vop(lambda v: v.tensor_tensor(q_t[:, sl], src_ap, q_inv[:, sl], A.mult))
            vop(lambda v: v.tensor_scalar(q_l[:, sl], q_t[:, sl], 0.5, 1024.25 + 1e-7, A.mult, A.add))
            vop(lambda v: v.tensor_copy(q_d[:, sl], q_l[:, sl]))
            for k in (2048, 1024, 512):
                lvl(q_d, q_c, k, n)
            vop(lambda v: v.tensor_scalar(q_c[:, sl], q_d[:, sl], 256.0, -256.0, A.is_ge, A.mult))
            vop(lambda v: v.tensor_tensor(q_w[:, sl], q_d[:, sl], q_c[:, sl], A.add))
            for k in (128, 64, 32, 16, 8, 4, 2, 1):
                lvl(q_w, q_c, k, n)
            vop(lambda v: v.tensor_scalar(q_c[:, sl], q_w[:, sl], 0.5 + 3e-7, None, A.is_ge))
            vop(lambda v: v.tensor_tensor(q_d[:, sl], q_l[:, sl], q_w[:, sl], A.subtract))
            vop(lambda v: v.scalar_tensor_tensor(q_d[:, sl], q_d[:, sl], 2.0, q_c[:, sl], A.mult, A.add))
            vop(lambda v: v.scalar_tensor_tensor(q_t[:, sl], q_d[:, sl], -2048.0, q_scl[:, sl], A.add, A.mult))
            wait("scalar", vC, vC.n)
            sop(lambda s: s.activation(q_d[:, sl], q_t[:, sl], AF.Abs, bias=0.0, scale=1.0))
            sop(lambda s: s.activation(q_f16[:, sl], q_d[:, sl], AF.Exp, bias=c_ap(6), scale=1.0e30))
            wait("vector", sC, sC.n)
            vop(lambda v: v.scalar_tensor_tensor(dst_ap, q_f16[:, sl], 1.0, q_t[:, sl], A.add, A.mult))
            return
        man3 = mode == "w"
        cbias_inv = 8 if man3 else 4
        cbias_scl = 9 if man3 else 5
        cmin = -6.0 if man3 else -14.0
        covf = 3 if man3 else 6
        shift = 32.0 if man3 else 2048.0
        cc = 1.0 / (2.0 * LN2_)
        man_ = 3.0 if man3 else 10.0
        sop(lambda s: s.activation(q_t[:, sl], src_ap, AF.Square, bias=0.0, scale=1.0))
        sop(lambda s: s.activation(q_l[:, sl], q_t[:, sl], AF.Ln, bias=c_ap(0), scale=1.0))
        wait("vector", sC, sC.n)
        vop(lambda v: v.tensor_scalar(q_i0[:, sl], q_l[:, sl], cc, 63.5, A.mult, A.add))
        vop(lambda v: v.scalar_tensor_tensor(q_dm[:, sl], q_l[:, sl], cc, q_i0[:, sl], A.mult, A.subtract))
        vop(lambda v: v.scalar_tensor_tensor(q_ec[:, sl], q_dm[:, sl], -63.0, q_i0[:, sl], A.is_ge, A.add))
        vop(lambda v: v.tensor_scalar(q_i0[:, sl], q_ec[:, sl], -64.0 - man_, cmin - man_, A.add, A.max))
        wait("scalar", vC, vC.n)
        einc("scalar", sC, lambda s: s.activation(q_inv[:, sl], q_i0[:, sl], AF.Exp, bias=0.0, scale=-LN2_))
        sop(lambda s: s.activation(q_scl[:, sl], q_i0[:, sl], AF.Exp, bias=0.0, scale=LN2_))
        wait("vector", sC, sC.n)
        vop(lambda v: v.tensor_tensor(q_t[:, sl], src_ap, q_inv[:, sl], A.mult))
        vop(lambda v: v.tensor_scalar(q_t[:, sl], q_t[:, sl], shift, None, A.add))
        vop(lambda v: v.tensor_scalar(q_i5[:, sl], q_t[:, sl], 0.5, -0.5, A.mult, A.add))
        vop(lambda v: v.scalar_tensor_tensor(q_d[:, sl], q_t[:, sl], 0.5, q_i5[:, sl], A.mult, A.subtract))
        vop(lambda v: v.scalar_tensor_tensor(q_d[:, sl], q_d[:, sl], 1.0, q_i5[:, sl], A.is_ge, A.add))
        vop(lambda v: v.scalar_tensor_tensor(q_u2[:, sl], q_d[:, sl], -2.0, q_t[:, sl], A.mult, A.add))
        vop(lambda v: v.tensor_scalar(q_a[:, sl], q_u2[:, sl], 0.5 + 1e-9, None, A.is_gt))
        vop(lambda v: v.tensor_tensor(q_d[:, sl], q_t[:, sl], q_u2[:, sl], A.subtract))
        vop(lambda v: v.tensor_tensor(q_d[:, sl], q_d[:, sl], q_a[:, sl], A.add))
        vop(lambda v: v.tensor_scalar(q_a[:, sl], q_u2[:, sl], 1.5 - 1e-9, None, A.is_ge))
        vop(lambda v: v.tensor_tensor(q_d[:, sl], q_d[:, sl], q_a[:, sl], A.add))
        vop(lambda v: v.scalar_tensor_tensor(q_t[:, sl], q_d[:, sl], -shift, q_scl[:, sl], A.add, A.mult))
        wait("scalar", vC, vC.n)
        sop(lambda s: s.activation(q_d[:, sl], q_t[:, sl], AF.Abs, bias=0.0, scale=1.0))
        sop(lambda s: s.activation(q_f[:, sl], q_d[:, sl], AF.Exp, bias=c_ap(covf), scale=1.0e30))
        wait("vector", sC, sC.n)
        vop(lambda v: v.scalar_tensor_tensor(dst_ap, q_f[:, sl], 1.0, q_t[:, sl], A.add, A.mult))

    def quant_chain_h_list(items, pre_emit=None):
        NBat = len(items)
        vm1, vm2, vm3, sm1, sm2, sm3 = {}, {}, {}, {}, {}, {}
        QL = (q_l, q_l2)
        QT = (q_t16, q_t162)
        sl = slice(0, QB)
        entry_v = vC.n
        if pre_emit is None:
            pre_emit = lambda i, vm2: None

        def S1(i):
            ql = QL[i % 2]
            s0 = items[i][0]
            wait("scalar", vC, vm1.get(i - 2, entry_v))
            sop(lambda s, ql=ql, s0=s0: s.activation(ql[:, sl], hbuf[:, s0:s0 + QB], AF.Ln, bias=c_ap(0), scale=1.0))
            sm1[i] = sC.n

        def V1(i):
            ql = QL[i % 2]
            wait("vector", sC, max(sm1[i], sm2.get(i - 1, 0)))
            vop(lambda v, ql=ql: v.tensor_scalar(q_w0[:, sl], ql[:, sl], 1.0 / LN2_, 6.0, A.mult, A.add))
            vop(lambda v: v.tensor_scalar(q_w0[:, sl], q_w0[:, sl], 0.0, None, A.max))
            lvl2(q_w0, q_w, q_c, 8, QB)
            for k in (4, 2, 1):
                lvl(q_w, q_c, k, QB)
            vop(lambda v: v.scalar_tensor_tensor(q_ec[:, sl], q_w0[:, sl], -9.0, q_w[:, sl], A.add, A.subtract))
            vm1[i] = vC.n

        def S2(i):
            wait("scalar", vC, vm1[i])
            einc("scalar", sC, lambda s: s.activation(q_inv[:, sl], q_ec[:, sl], AF.Exp, bias=0.0, scale=-LN2_))
            sop(lambda s: s.activation(q_scl[:, sl], q_ec[:, sl], AF.Exp, bias=0.0, scale=LN2_))
            sm2[i] = sC.n

        def V2(i):
            qt = QT[i % 2]
            s0 = items[i][0]
            wait("vector", sC, sm2[i])
            vop(lambda v, qt=qt, s0=s0: v.tensor_tensor(qt[:, sl], hbuf[:, s0:s0 + QB], q_inv[:, sl], A.mult))
            vop(lambda v, qt=qt: v.tensor_scalar(q_w0[:, sl], qt[:, sl], 0.5, 0.25 + 1e-13, A.mult, A.add))
            lvl2(q_w0, q_w, q_c, 8, QB)
            for k in (4, 2, 1):
                lvl(q_w, q_c, k, QB)
            vop(lambda v: v.tensor_scalar(q_c[:, sl], q_w[:, sl], 0.5 + 1e-12, None, A.is_ge))
            vop(lambda v: v.tensor_tensor(q_r[:, sl], q_w0[:, sl], q_w[:, sl], A.subtract))
            vop(lambda v: v.scalar_tensor_tensor(q_r[:, sl], q_r[:, sl], 2.0, q_c[:, sl], A.mult, A.add))
            vop(lambda v, qt=qt: v.tensor_tensor(qt[:, sl], q_r[:, sl], q_scl[:, sl], A.mult))
            vm2[i] = vC.n

        def S3(i):
            qt = QT[i % 2]
            wait("scalar", vC, vm2[i])
            sop(lambda s, qt=qt: s.activation(q_f16[:, sl], qt[:, sl], AF.Exp, bias=c_ap(3), scale=1.0e30))
            sm3[i] = sC.n

        def V3(i):
            qt = QT[i % 2]
            d0 = items[i][1]
            wait("vector", sC, sm3[i])
            vop(lambda v, qt=qt, d0=d0: v.scalar_tensor_tensor(
                hq8[:, d0:d0 + QB], q_f16[:, sl], 1.0, qt[:, sl], A.add, A.mult))
            vm3[i] = vC.n

        pre_emit(0, vm2)
        S1(0)
        for i in range(NBat + 1):
            if 0 <= i - 1 < NBat:
                V2(i - 1)
            if i < NBat:
                V1(i)
            if 0 <= i - 1 < NBat:
                S3(i - 1)
                V3(i - 1)
            if i + 1 < NBat:
                pre_emit(i + 1, vm2)
                S1(i + 1)
            if i < NBat:
                S2(i)
        return [vm2[i] for i in range(NBat)]

    def quant_chain_o_list(cols, pre_emit=None):
        NBat = len(cols)
        vm1, vm2, vm3, sm1, sm2, sm3 = {}, {}, {}, {}, {}, {}
        QL = (q_l, q_l2)
        QT = (q_t, q_t2)
        QA = (q_t16, q_t162)
        sl = slice(0, QB)
        entry_v = vC.n
        if pre_emit is None:
            pre_emit = lambda i, vm2: None

        def S1(i):
            ql, qa = QL[i % 2], QA[i % 2]
            s0 = cols[i]
            wait("scalar", vC, vm2.get(i - 2, entry_v))
            sop(lambda s, qa=qa, s0=s0: s.activation(qa[:, sl], qo[:, s0:s0 + QB], AF.Abs, bias=0.0, scale=1.0))
            sop(lambda s, ql=ql, qa=qa: s.activation(ql[:, sl], qa[:, sl], AF.Ln, bias=c_ap(0), scale=1.0))
            sm1[i] = sC.n

        def V1(i):
            ql = QL[i % 2]
            wait("vector", sC, max(sm1[i], sm2.get(i - 1, 0)))
            vop(lambda v, ql=ql: v.tensor_scalar(q_w0[:, sl], ql[:, sl], 1.0 / LN2_, 14.0, A.mult, A.add))
            vop(lambda v: v.tensor_scalar(q_w0[:, sl], q_w0[:, sl], 0.0, 30.0, A.max, A.min))
            lvl2(q_w0, q_w, q_c, 16, QB)
            for k in (8, 4, 2, 1):
                lvl(q_w, q_c, k, QB)
            vop(lambda v: v.scalar_tensor_tensor(q_ec[:, sl], q_w0[:, sl], -24.0, q_w[:, sl], A.add, A.subtract))
            vm1[i] = vC.n

        def S2(i):
            wait("scalar", vC, vm1[i])
            einc("scalar", sC, lambda s: s.activation(q_inv[:, sl], q_ec[:, sl], AF.Exp, bias=0.0, scale=-LN2_))
            sop(lambda s: s.activation(q_scl[:, sl], q_ec[:, sl], AF.Exp, bias=0.0, scale=LN2_))
            sm2[i] = sC.n

        def V2(i):
            ql, qt = QL[i % 2], QT[i % 2]
            s0 = cols[i]
            wait("vector", sC, sm2[i])
            vop(lambda v, qt=qt, s0=s0: v.tensor_tensor(qt[:, sl], qo[:, s0:s0 + QB], q_inv[:, sl], A.mult))
            vop(lambda v, ql=ql, qt=qt: v.tensor_scalar(ql[:, sl], qt[:, sl], 0.5, 1024.25 + 1e-7, A.mult, A.add))
            lvl2(ql, q_d, q_c, 2048, QB)
            for k in (1024, 512):
                lvl(q_d, q_c, k, QB)
            vop(lambda v: v.tensor_scalar(q_c[:, sl], q_d[:, sl], 256.0, -256.0, A.is_ge, A.mult))
            vop(lambda v: v.tensor_tensor(q_w[:, sl], q_d[:, sl], q_c[:, sl], A.add))
            for k in (128, 64, 32, 16, 8, 4, 2, 1):
                lvl(q_w, q_c, k, QB)
            vop(lambda v: v.tensor_scalar(q_c[:, sl], q_w[:, sl], 0.5 + 3e-7, None, A.is_ge))
            vop(lambda v, ql=ql: v.tensor_tensor(q_d[:, sl], ql[:, sl], q_w[:, sl], A.subtract))
            vop(lambda v: v.scalar_tensor_tensor(q_d[:, sl], q_d[:, sl], 2.0, q_c[:, sl], A.mult, A.add))
            vop(lambda v, qt=qt: v.scalar_tensor_tensor(qt[:, sl], q_d[:, sl], -2048.0, q_scl[:, sl], A.add, A.mult))
            vm2[i] = vC.n

        def S3(i):
            qt, qa = QT[i % 2], QA[i % 2]
            wait("scalar", vC, vm2[i])
            sop(lambda s, qa=qa, qt=qt: s.activation(qa[:, sl], qt[:, sl], AF.Abs, bias=0.0, scale=1.0))
            sop(lambda s, qa=qa: s.activation(q_f16[:, sl], qa[:, sl], AF.Exp, bias=c_ap(6), scale=1.0e30))
            sm3[i] = sC.n

        def V3(i):
            qt = QT[i % 2]
            d0 = cols[i]
            wait("vector", sC, sm3[i])
            vop(lambda v, qt=qt, d0=d0: v.scalar_tensor_tensor(
                qo[:, d0:d0 + QB], q_f16[:, sl], 1.0, qt[:, sl], A.add, A.mult))
            vm3[i] = vC.n

        pre_emit(0, vm2)
        S1(0)
        for i in range(NBat + 1):
            if 0 <= i - 1 < NBat:
                V2(i - 1)
            if i < NBat:
                V1(i)
            if 0 <= i - 1 < NBat:
                S3(i - 1)
                V3(i - 1)
            if i + 1 < NBat:
                pre_emit(i + 1, vm2)
                S1(i + 1)
            if i < NBat:
                S2(i)

    gCast = gpP.group()
    einc("gpsimd", gCast, lambda g: g.dma_start(out=qkvscr[:, :], in_=wqkv_h[:, :]))
    einc("gpsimd", gCast, lambda g: g.dma_start(out=projscr[:, :], in_=wproj_h[:, :]))
    einc("gpsimd", gCast, lambda g: g.dma_start(out=w1scr[:, :], in_=w1_h[:, :]))
    D_CAST = gCast.n
    gSmall = gpP.group()
    einc("gpsimd", gSmall, lambda g: g.dma_start(
        out=b1sb[:, :], in_=b1_h.rearrange("(a b) -> b a", b=128),
        allow_slow_non_contiguous=True))
    einc("gpsimd", gSmall, lambda g: g.dma_start(
        out=bprojrep[:, :], in_=bproj_h.unsqueeze(0).broadcast_to([128, C]),
        allow_slow_non_contiguous=True))
    einc("gpsimd", gSmall, lambda g: g.dma_start(
        out=bqrep[:, :], in_=b2_h.unsqueeze(0).broadcast_to([128, C]),
        allow_slow_non_contiguous=True))
    einc("gpsimd", gSmall, lambda g: g.dma_start(out=identT[:, :], in_=ident_h[:, :]))
    D_SMALL = gSmall.n

    gWtr = syP.group()
    wait("sync", gCast, D_CAST)
    for j in range(8):
        einc("sync", gWtr, lambda s, j=j: s.dma_start_transpose(
            wqT[j][:, :], qkvscr[0:C, j * 128:(j + 1) * 128]))
        einc("sync", gWtr, lambda s, j=j: s.dma_start_transpose(
            wkvT[j][:, :], qkvscr[C:3 * C, j * 128:(j + 1) * 128]))
        einc("sync", gWtr, lambda s, j=j: s.dma_start_transpose(
            wpT[j][:, :], projscr[0:C, j * 128:(j + 1) * 128]))
    D_WTR = gWtr.n
    gW1 = syP.group()
    for j in range(8):
        einc("sync", gW1, lambda s, j=j: s.dma_start_transpose(
            qsT[:, :], w1scr[0:HID // 2, j * 128:(j + 1) * 128]))
        einc("sync", gW1, lambda s, j=j: s.dma_start_transpose(
            qsT2[:, :], w1scr[HID // 2:HID, j * 128:(j + 1) * 128]))
        wait("sync", gW1, gW1.n)
        einc("sync", gW1, lambda s, j=j: s.dma_start(
            out=w1T_scr[j * 128:(j + 1) * 128, 0:HID // 2], in_=qsT[:, :]))
        einc("sync", gW1, lambda s, j=j: s.dma_start(
            out=w1T_scr[j * 128:(j + 1) * 128, HID // 2:HID], in_=qsT2[:, :]))
        wait("sync", gW1, gW1.n)
    D_W1T = gW1.n

    def ln_stats_apply(col, src_ap, dst_ap, cache):
        wait("vector", vC, vC.n)
        einc("vector", vC, lambda v: v.bn_stats(bnbuf[:, 0:6], src_ap[:, 0:512]))
        einc("vector", vC, lambda v: v.bn_stats(bnbuf[:, 6:12], src_ap[:, 512:1024]))
        wait("vector", vC, vC.n)
        einc("vector", vC, lambda v: v.bn_aggr(st4[:, 0:2], bnbuf[:, :]))
        wait("scalar", vC, vC.n)
        einc("scalar", sC, lambda s: s.activation(st4[:, 2:3], st4[:, 1:2], AF.Sqrt, bias=c_ap(7), scale=1.0))
        wait("vector", sC, sC.n)
        if cache:
            rap, bap = statR[:, col:col + 1], statB[:, col:col + 1]
        else:
            rap, bap = st4[:, 3:4], st4[:, 4:5]
        einc("vector", vC, lambda v: v.reciprocal(rap, st4[:, 2:3]))
        wait("vector", vC, vC.n)
        einc("vector", vC, lambda v: v.scalar_tensor_tensor(bap, st4[:, 0:1], -1.0, rap, A.mult, A.mult))
        wait("scalar", vC, vC.n)
        einc("scalar", sC, lambda s: s.activation(dst_ap, src_ap, AF.Identity, bias=bap, scale=rap))

    def ln_apply_cached(col, src_ap, dst_ap):
        einc("scalar", sC, lambda s: s.activation(
            dst_ap, src_ap, AF.Identity,
            bias=statB[:, col:col + 1], scale=statR[:, col:col + 1]))

    lnT_tiles = [(lnT, 0), (lnT, CH), (lnT, 2 * CH), (lnT, 3 * CH),
                 (lnT2, 0), (lnT2, CH), (lnT2, 2 * CH), (lnT2, 3 * CH)]
    qsT_tiles = [(qsT, 0), (qsT, CH), (qsT, 2 * CH), (qsT, 3 * CH),
                 (qsT2, 0), (qsT2, CH), (qsT2, 2 * CH), (qsT2, 3 * CH)]

    def lnT_ap(j, a, b):
        tns, coff = lnT_tiles[j]
        return tns[:, coff + a: coff + b]

    def qsT_ap(j, a, b, r0=0, r1=128):
        tns, coff = qsT_tiles[j]
        return tns[r0:r1, coff + a: coff + b]

    tr_ev = {"prev": 0}
    def transpose_chunk(tiles, src):
        s_pre = sC.n
        t_last = 0
        for j in range(8):
            tns, coff = tiles[j]
            wait("tensor", sC, max(tr_ev["prev"], s_pre))
            for subq in range(4):
                ins = lambda t, subq=subq, j=j, src=src: t.matmul(
                    p_tr[:, subq * 128:(subq + 1) * 128],
                    src[:, subq * C + j * 128: subq * C + (j + 1) * 128],
                    identT[:, :], start=(subq == 0), stop=(subq == 3))
                if subq == 3:
                    einc("tensor", tC, lambda t, ins=ins: ins(t))
                else:
                    emit("tensor", ins)
            t_last = tC.n
            wait("scalar", tC, t_last)
            einc("scalar", sC, lambda s, tns=tns, coff=coff: s.activation(
                tns[:, coff:coff + 512], p_tr[:, :], AF.Identity, bias=0.0, scale=1.0))
            tr_ev["prev"] = sC.n
        return (t_last, sC.n)

    wait("scalar", gW1, D_W1T)
    wait("vector", gW1, D_W1T)
    gW2L = gpP.group()
    gW2S = gpP.group()

    def emit_w2_tile(i):
        d_tile = gW2L.n + 16
        einc("gpsimd", gW2L, lambda g, i=i: g.dma_start(
            out=w2f[:, :], in_=w2_h[i * 128:(i + 1) * 128, :]))
        wait("scalar", gW2L, d_tile)
        wait("vector", gW2L, d_tile)
        wait("vector", gW2S, gW2S.n)
        for hh in range(HID // QB):
            sl = slice(hh * QB, (hh + 1) * QB)
            wq_dst = qsT[:, (hh % 2) * QB:(hh % 2 + 1) * QB] if hh < 2 else \
                qsT2[:, (hh % 2) * QB:(hh % 2 + 1) * QB]
            quant_chain(QB, w2f[:, sl], wq_dst, "w")
        v_tile = vC.n
        wait("gpsimd", vC, v_tile)
        einc("gpsimd", gW2S, lambda g, i=i: g.dma_start(
            out=wqscr[i * 128:(i + 1) * 128, 0:HID // 2], in_=qsT[:, :]))
        einc("gpsimd", gW2S, lambda g, i=i: g.dma_start(
            out=wqscr[i * 128:(i + 1) * 128, HID // 2:HID], in_=qsT2[:, :]))

    s_ln1_prev = 0
    t_trA_prev, s_trA = 0, 0
    ev_done_t = 0
    for chk in range(NCH):
        gX = syP.group()
        d_x = gX.n + 64
        wait("sync", sC, s_ln1_prev)
        for subq in range(4):
            einc("sync", gX, lambda s, chk=chk, subq=subq: s.dma_start(
                out=xc[:, subq * C:(subq + 1) * C],
                in_=x_h[chk * CH + subq * 128: chk * CH + (subq + 1) * 128, :]))
        wait("vector", gX, d_x)
        wait("scalar", gX, d_x)
        wait("scalar", tC, t_trA_prev)
        wait("tensor", gSmall, D_SMALL if chk == 0 else 0)
        for sub in range(4):
            ln_stats_apply(chk * 4 + sub, xc[:, sub * C:(sub + 1) * C],
                           lnc[:, sub * C:(sub + 1) * C], cache=True)
        s_ln1_prev = sC.n
        wait("scalar", sC, s_ln1_prev)
        t_trA_prev, s_trA = transpose_chunk(lnT_tiles, lnc)
        for sub in range(4):
            wait("tensor", sC, s_trA)
            wait("tensor", gWtr, D_WTR)
            for c in range(8):
                for f in range(4):
                    ins = lambda t, c=c, f=f, sub=sub: t.matmul(
                        p_kv[:, f * 512:(f + 1) * 512],
                        lnT_ap(c, sub * 128, (sub + 1) * 128),
                        wkvT[c][:, f * 512:(f + 1) * 512],
                        start=(c == 0), stop=(c == 7))
                    if c == 7 and f == 3:
                        einc("tensor", tC, lambda t, ins=ins: ins(t))
                    else:
                        emit("tensor", ins)
            t_kv = tC.n
            wait("scalar", tC, max(t_kv, ev_done_t))
            einc("scalar", sC, lambda s: s.activation(ek[:, :], p_kv[:, 0:C], AF.Exp, bias=0.0, scale=1.0))
            s_ek = sC.n
            wait("vector", tC, max(t_kv, ev_done_t))
            einc("vector", vC, lambda v: v.tensor_copy(vv[:, :], p_kv[:, C:2 * C]))
            v_vv = vC.n
            wait("tensor", sC, s_ek)
            wait("tensor", vC, v_vv)
            first = (chk == 0) and (sub == 0)
            last = (chk == NCH - 1) and (sub == 3)
            for g in range(8):
                emit("tensor", lambda t, g=g, first=first, last=last: t.matmul(
                    p_ctx[:, g * 128:(g + 1) * 128], ek[:, g * 128:(g + 1) * 128],
                    vv[:, g * 128:(g + 1) * 128],
                    start=(first and g % 4 == 0), stop=(last and g % 4 == 3)))
                ins2 = lambda t, g=g, first=first, last=last: t.matmul(
                    p_den[:, g:g + 1], ek[:, g * 128:(g + 1) * 128],
                    ones_col[:, 0:1], start=(first and g == 0), stop=(last and g == 7))
                if g == 7:
                    einc("tensor", tC, lambda t, ins2=ins2: ins2(t))
                else:
                    emit("tensor", ins2)
            ev_done_t = tC.n
        emit_w2_tile(chk)
    T_PHA = tC.n

    wait("vector", tC, T_PHA)
    einc("vector", vC, lambda v: v.reciprocal(rdq[:, 0:8], p_den[:, 0:8]))
    v_rd = vC.n
    V_ENDA = vC.n
    wait("scalar", vC, v_rd)
    for g in range(8):
        einc("scalar", sC, lambda s, g=g: s.activation(
            ctxn[g][:, :], p_ctx[:, g * 128:(g + 1) * 128], AF.Identity,
            bias=0.0, scale=rdq[:, g:g + 1]))
    S_CTXN = sC.n

    D_WQ = gW2S.n
    gWq = syP.group()
    gWq8 = gpP.group()
    wait("sync", gW2S, D_WQ)
    for j in range(32):
        einc("sync", gWq, lambda s, j=j: s.dma_start_transpose(
            qsT[:, 0:C], wqscr[0:C, j * 128:(j + 1) * 128]))
        wait("vector", gWq, gWq.n)
        wait("vector", gWq8, gWq8.n)
        einc("vector", vC, lambda v: v.tensor_copy(wq8stg[:, :], qsT[:, 0:C]))
        wait("sync", vC, vC.n)
        wait("gpsimd", vC, vC.n)
        einc("gpsimd", gWq8, lambda g, j=j: g.dma_start(
            out=wqT8h[j * 128:(j + 1) * 128, :], in_=wq8stg[:, :]))
    V_WQT = vC.n
    D_WQ8 = gWq8.n

    wait("scalar", gSmall, D_SMALL)
    wait("vector", gSmall, D_SMALL)
    quant_chain(C, bqrep[:, :], bqrep[:, :], "b")
    V_BQ = vC.n
    S_PREP = sC.n

    wait("scalar", gSmall, D_SMALL)
    wait("scalar", gWq, gWq.n)
    wait("scalar", gW2S, D_WQ)
    wait("scalar", vC, V_WQT)
    t_wq_free = T_PHA
    c1 = 1.0 / LN2_
    v_prev_fin = 0
    t_tr2_prev = 0
    t_qsT_prev, s_qsT_prev = 0, 0
    g_out_prev = None
    t_m2_prev = T_PHA
    t_pj_prev = 0
    t_m1_done = [T_PHA, T_PHA]
    h_prev_marks = None

    def prefetch_xb(chk, t_gate):
        gXB = gpP.group()
        wait("gpsimd", tC, t_gate)
        for subq in range(4):
            tns = lnT if subq < 2 else lnT2
            c0 = (subq % 2) * C
            einc("gpsimd", gXB, lambda g, chk=chk, subq=subq, tns=tns, c0=c0: g.dma_start(
                out=tns[:, c0:c0 + C],
                in_=x_h[chk * CH + subq * 128: chk * CH + (subq + 1) * 128, :]))
        return gXB, gXB.n
    d_out_prev = 0
    gXB, d_xb = prefetch_xb(0, T_PHA)
    for chk in range(NCH):
        gX = syP.group()
        d_x = gX.n + 64
        wait("sync", vC, v_prev_fin)
        wait("sync", sC, S_CTXN if chk == 0 else 0)
        for subq in range(4):
            einc("sync", gX, lambda s, chk=chk, subq=subq: s.dma_start(
                out=xc[:, subq * C:(subq + 1) * C],
                in_=x_h[chk * CH + subq * 128: chk * CH + (subq + 1) * 128, :]))
        wait("scalar", gXB, d_xb)
        wait("scalar", vC, V_ENDA)
        wait("scalar", tC, t_tr2_prev)
        for sub in range(4):
            xtns = lnT if sub < 2 else lnT2
            ln_apply_cached(chk * 4 + sub, xtns[:, (sub % 2) * C:(sub % 2 + 1) * C],
                            lnc[:, sub * C:(sub + 1) * C])
        s_ln1b = sC.n
        wait("scalar", sC, s_ln1b)
        t_tr1, s_tr1 = transpose_chunk(lnT_tiles, lnc)
        s_eq = 0
        for sub in range(4):
            wait("tensor", sC, max(S_CTXN, s_eq, s_tr1))
            wait("tensor", vC, max(V_WQT, V_BQ))
            for c in range(8):
                for f in range(2):
                    ins = lambda t, c=c, f=f, sub=sub: t.matmul(
                        p_q[:, f * 512:(f + 1) * 512],
                        lnT_ap(c, sub * 128, (sub + 1) * 128),
                        wqT[c][:, f * 512:(f + 1) * 512],
                        start=(c == 0), stop=(c == 7))
                    if c == 7 and f == 1:
                        einc("tensor", tC, lambda t, ins=ins: ins(t))
                    else:
                        emit("tensor", ins)
            t_q = tC.n
            wait("scalar", tC, t_q)
            wait("scalar", tC, t_qsT_prev)
            einc("scalar", sC, lambda s, sub=sub: s.activation(
                eq_[:, sub * C:(sub + 1) * C], p_q[:, :], AF.Exp, bias=0.0, scale=1.0))
            s_eq = sC.n
            wait("vector", sC, s_eq)
            einc("vector", vC, lambda v, sub=sub: v.tensor_reduce(
                dq[:, :].unsqueeze(2),
                eq_[:, sub * C:(sub + 1) * C].rearrange("p (h d) -> p h d", h=NH),
                AX.X, A.add))
            wait("vector", vC, vC.n)
            einc("vector", vC, lambda v: v.reciprocal(rdq[:, 0:16], dq[:, :]))
            wait("vector", vC, vC.n)
            for hh in range(NH):
                ins = lambda v, hh=hh, sub=sub: v.tensor_scalar(
                    eq_[:, sub * C + hh * HD: sub * C + (hh + 1) * HD],
                    eq_[:, sub * C + hh * HD: sub * C + (hh + 1) * HD],
                    rdq[:, hh:hh + 1], None, A.mult)
                if hh == NH - 1:
                    einc("vector", vC, lambda v, ins=ins: ins(v))
                else:
                    emit("vector", ins)
            v_qs = vC.n
            wait("scalar", tC, t_pj_prev)
            for jg in range(2):
                wait("tensor", vC, v_qs)
                wait("tensor", sC, tr_ev["prev"])
                for q in range(4):
                    j = jg * 4 + q
                    ins = lambda t, q=q, j=j, sub=sub: t.matmul(
                        p_tr[:, q * 128:(q + 1) * 128],
                        eq_[:, sub * C + j * 128: sub * C + (j + 1) * 128],
                        identT[:, :], start=(q == 0), stop=(q == 3))
                    if q == 3:
                        einc("tensor", tC, lambda t, ins=ins: ins(t))
                    else:
                        emit("tensor", ins)
                wait("scalar", tC, tC.n)
                for q in range(4):
                    tns, coff = qsT_tiles[jg * 4 + q]
                    einc("scalar", sC, lambda s, tns=tns, coff=coff, q=q, sub=sub: s.activation(
                        tns[:, coff + sub * 128: coff + (sub + 1) * 128],
                        p_tr[:, q * 128:(q + 1) * 128], AF.Identity, bias=0.0, scale=1.0))
                tr_ev["prev"] = sC.n
        t_qsT_prev, s_qsT_prev = tC.n, sC.n
        wait("tensor", sC, s_qsT_prev)
        for hg in range(4):
            for hh in range(4):
                h = hg * 4 + hh
                g = h // 2
                r0 = (h % 2) * 64
                ins = lambda t, g=g, r0=r0, hh=hh: t.matmul(
                    p_m1[0:64, hh * CH:(hh + 1) * CH], ctxn[g][r0:r0 + 64, r0:r0 + 64],
                    qsT_ap(g, 0, CH, r0, r0 + 64), start=True, stop=True)
                if hh == 3:
                    einc("tensor", tC, lambda t, ins=ins: ins(t))
                else:
                    emit("tensor", ins)
            t_at = tC.n
            wait("vector", tC, t_at)
            for hh in range(4):
                h = hg * 4 + hh
                g = h // 2
                r0 = (h % 2) * 64
                tns, coff = qsT_tiles[g]
                ins = lambda v, tns=tns, coff=coff, r0=r0, hh=hh: v.tensor_copy(
                    tns[r0:r0 + 64, coff:coff + CH], p_m1[0:64, hh * CH:(hh + 1) * CH])
                if hh == 3:
                    einc("vector", vC, lambda v, ins=ins: ins(v))
                else:
                    emit("vector", ins)
            wait("tensor", vC, vC.n)
        v_attnT = vC.n
        for sub in range(4):
            wait("tensor", vC, v_attnT)
            for c in range(8):
                for f in range(2):
                    ins = lambda t, c=c, f=f, sub=sub: t.matmul(
                        p_q[:, f * 512:(f + 1) * 512],
                        qsT_ap(c, sub * 128, (sub + 1) * 128),
                        wpT[c][:, f * 512:(f + 1) * 512],
                        start=(c == 0), stop=(c == 7))
                    if c == 7 and f == 1:
                        einc("tensor", tC, lambda t, ins=ins: ins(t))
                    else:
                        emit("tensor", ins)
            t_pj = tC.n
            wait("vector", tC, t_pj)
            wait("vector", gX, d_x)
            wait("vector", gSmall, D_SMALL)
            if g_out_prev is not None:
                wait("vector", g_out_prev, d_out_prev)
            einc("vector", vC, lambda v, sub=sub: v.tensor_tensor(
                yc[:, sub * C:(sub + 1) * C], p_q[:, :], bprojrep[:, :], A.add))
            wait("vector", vC, vC.n)
            einc("vector", vC, lambda v, sub=sub: v.tensor_tensor(
                yc[:, sub * C:(sub + 1) * C], yc[:, sub * C:(sub + 1) * C],
                xc[:, sub * C:(sub + 1) * C], A.add))
            wait("tensor", vC, vC.n)
            ln_stats_apply(0, yc[:, sub * C:(sub + 1) * C],
                           lnc[:, sub * C:(sub + 1) * C], cache=False)
        s_ln2 = sC.n
        t_pj_prev = tC.n
        t_tr2, s_tr2 = transpose_chunk(lnT_tiles, lnc)
        t_tr2_prev = t_tr2
        wait("scalar", tC, t_m2_prev)
        wait("vector", tC, t_m2_prev)
        def emit_hb_block(j, prev_marks, cvm2):
            hb = j
            half = j // 4
            buf = hb % 2
            gW = gpP.group()
            d_w1 = gW.n + 128
            wait("gpsimd", tC, t_m1_done[buf])
            for cq in range(8):
                einc("gpsimd", gW, lambda g, hb=hb, buf=buf, cq=cq: g.dma_start(
                    out=w1buf[:, buf * HID + cq * 512: buf * HID + (cq + 1) * 512],
                    in_=w1T_scr[cq * 128:(cq + 1) * 128, hb * 512:(hb + 1) * 512]))
            wait("tensor", gW, d_w1)
            wait("tensor", sC, s_tr2)
            for c in range(8):
                for k in range(4):
                    ins = lambda t, c=c, k=k, buf=buf: t.matmul(
                        p_m1[:, k * 512:(k + 1) * 512],
                        w1buf[:, buf * HID + c * 512 + k * 128: buf * HID + c * 512 + (k + 1) * 128],
                        lnT_ap(c, 0, CH), start=(c == 0), stop=(c == 7))
                    if c == 7 and k == 3:
                        einc("tensor", tC, lambda t, ins=ins: ins(t))
                    else:
                        emit("tensor", ins)
            t_m1_done[buf] = tC.n
            wait("scalar", tC, t_m1_done[buf])
            if half == 0:
                if prev_marks is not None:
                    wait("scalar", vC, prev_marks[8 + 2 * j + 1])
            else:
                wait("scalar", vC, cvm2[2 * (j - 4) + 1])
            for k in range(4):
                ht = hb * 4 + k
                ins = lambda s, k=k, ht=ht, half=half: s.activation(
                    hbuf[:, (ht - 16 * half) * CH:(ht - 16 * half + 1) * CH],
                    p_m1[:, k * 512:(k + 1) * 512],
                    AF.Relu, bias=b1sb[:, ht:ht + 1], scale=1.0)
                if k == 3:
                    einc("scalar", sC, lambda s, ins=ins: ins(s))
                else:
                    emit("scalar", ins)
            wait("tensor", sC, sC.n)

        done_blocks = set()
        pm_now = h_prev_marks

        def pre_h(i, cvm2, pm=pm_now, done=done_blocks):
            j = i // 2
            if j in done or j > 7:
                return
            done.add(j)
            emit_hb_block(j, pm, cvm2)
            wait("scalar", sC, sC.n)

        h_prev_marks = quant_chain_h_list(
            [(bb * QB, bb * QB) for bb in range(8)]
            + [(bb * QB, 8192 + bb * QB) for bb in range(8)],
            pre_emit=pre_h)
        v_hq8 = vC.n
        if chk + 1 < NCH:
            gXB, d_xb = prefetch_xb(chk + 1, max(t_m1_done[0], t_m1_done[1]))
        wq_hold = {"free": t_wq_free}

        def emit_sp(sp):
            wait("tensor", vC, v_hq8)
            t_hc = {}
            for hc in range(32):
                gate = wq_hold["free"] if hc < 4 else t_hc[hc - 4]
                wait("gpsimd", tC, gate)
                slot = wqS[hc % 4]
                einc("gpsimd", slot, lambda g, hc=hc: g.dma_start(
                    out=wq8buf[:, (hc % 4) * C:(hc % 4 + 1) * C],
                    in_=wqT8h[hc * 128:(hc + 1) * 128, :]))
                d_hc = slot.n
                wait("tensor", slot, d_hc)
                for sub in (2 * sp, 2 * sp + 1):
                    pq = p_q if sub % 2 == 0 else p_kv
                    for f in range(2):
                        ins = lambda t, hc=hc, f=f, sub=sub, pq=pq: t.matmul(
                            pq[:, f * 512:(f + 1) * 512],
                            hq8[:, hc * CH + sub * 128: hc * CH + (sub + 1) * 128],
                            wq8buf[:, (hc % 4) * C + f * 512: (hc % 4) * C + (f + 1) * 512],
                            start=(hc == 0), stop=(hc == 31))
                        if f == 1 and sub % 2 == 1:
                            einc("tensor", tC, lambda t, ins=ins: ins(t))
                        else:
                            emit("tensor", ins)
                t_hc[hc] = tC.n
            wq_hold["free"] = tC.n
            t_m2 = tC.n
            wait("vector", tC, t_m2)
            for sub in (2 * sp, 2 * sp + 1):
                pq = p_q if sub % 2 == 0 else p_kv
                einc("vector", vC, lambda v, sub=sub, pq=pq: v.tensor_tensor(
                    qo[:, sub * C:(sub + 1) * C], pq[:, 0:C], bqrep[:, :], A.add))
            wait("tensor", vC, vC.n)

        sp_done = set()

        def pre_o(i, cvm2):
            sp = i // 2
            if sp in sp_done or sp > 1:
                return
            sp_done.add(sp)
            emit_sp(sp)
            wait("scalar", vC, vC.n)

        quant_chain_o_list([bb * QB for bb in range(4 * C // QB)], pre_emit=pre_o)
        t_wq_free = wq_hold["free"]
        t_m2_prev = tC.n
        for sub in range(4):
            ins = lambda v, sub=sub: v.tensor_tensor(
                yc[:, sub * C:(sub + 1) * C], yc[:, sub * C:(sub + 1) * C],
                qo[:, sub * C:(sub + 1) * C], A.add)
            einc("vector", vC, lambda v, ins=ins: ins(v))
        v_prev_fin = vC.n
        wait("sync", vC, v_prev_fin)
        gO = syP.group()
        for subq in range(4):
            einc("sync", gO, lambda s, chk=chk, subq=subq: s.dma_start(
                out=out_h[chk * CH + subq * 128: chk * CH + (subq + 1) * 128, :],
                in_=yc[:, subq * C:(subq + 1) * C]))
        g_out_prev, d_out_prev = gO, gO.n

    for c in syP.ctrs + scP.ctrs + gpP.ctrs + wqS:
        if c.n > 0:
            wait("sync", c, c.n)

    with nc.Block() as block:
        @block.sync
        def _(e):
            for fn in prog["sync"]:
                fn(e)

        @block.scalar
        def _(e):
            for fn in prog["scalar"]:
                fn(e)

        @block.vector
        def _(e):
            for fn in prog["vector"]:
                fn(e)

        @block.tensor
        def _(e):
            for fn in prog["tensor"]:
                fn(e)

        @block.gpsimd
        def _(e):
            for fn in prog["gpsimd"]:
                fn(e)

    return nc


def kernel(L0_attn_x, L0_attn_w_qkv, L0_attn_w_proj, L0_attn_b_proj,
           L0_ffn_w1, L0_ffn_b1, L0_ffn_w2, L0_ffn_b2):
    x = np.ascontiguousarray(np.asarray(L0_attn_x, dtype=np.float32))
    w_qkv = np.ascontiguousarray(np.asarray(L0_attn_w_qkv, dtype=np.float32))
    w_proj = np.ascontiguousarray(np.asarray(L0_attn_w_proj, dtype=np.float32))
    b_proj = np.ascontiguousarray(np.asarray(L0_attn_b_proj, dtype=np.float32))
    w1 = np.ascontiguousarray(np.asarray(L0_ffn_w1, dtype=np.float32))
    b1 = np.ascontiguousarray(np.asarray(L0_ffn_b1, dtype=np.float32))
    w2 = np.ascontiguousarray(np.asarray(L0_ffn_w2, dtype=np.float32))
    b2 = np.ascontiguousarray(np.asarray(L0_ffn_b2, dtype=np.float32))

    nc = bass.Bass("TRN2", target_bir_lowering=False)
    with ExitStack() as es:
        build(nc, es)

    ident = np.eye(128, dtype=np.float32).astype(ml_dtypes.bfloat16)
    in_maps = [
        {"x": x[b], "w_qkv": w_qkv, "w_proj": w_proj, "b_proj": b_proj,
         "w1": w1, "b1": b1, "w2": w2, "b2": b2, "ident": ident}
        for b in range(8)
    ]
    res = run_bass_kernel_spmd(nc, in_maps, core_ids=list(range(8)))
    out = np.stack([np.asarray(r["out"], dtype=np.float32) for r in res.results],
                   axis=0)
    return out
